# version 11
# baseline (speedup 1.0000x reference)
import numpy as np
from contextlib import ExitStack

import concourse.bass as bass
import concourse.mybir as mybir
from concourse.bass_utils import run_bass_kernel_spmd

F32 = mybir.dt.float32
BF16 = mybir.dt.bfloat16
U8 = mybir.dt.uint8
AF = mybir.ActivationFunctionType
ALU = mybir.AluOpType
AX = mybir.AxisListType

S = 2048
D = 2048
NE = 8
NH = 16
HD = 128
NEG = -1.0e30
ISQ = 1.0 / float(np.sqrt(128.0))

N_SUB = 3


class Sem:
    def __init__(self, ctx, nc, name):
        self.h = ctx.enter_context(nc.semaphore(name))
        self.n = 0

    def inc(self, instr, k=1):
        instr.then_inc(self.h, k)
        self.n += k
        return self.n

    def step(self, eng, instr, k=1):
        n = self.inc(instr, k)
        eng.wait_ge(self.h, n)
        return n


def ffn_pair(nc, consts, dram, src_ap, dst_ap, tag, wmodes):
    TH = 1024
    xln_d, oacc_d = dram["xln"], dram["oacc"]
    wg_d, wd_d = dram["wg"], dram["wd"]
    wgc_d, wdc_d = dram["wgc"], dram["wdc"]
    identf32, self32, wrb, neginf8 = (consts["identf32"], consts["self32"],
                                      consts["wrb"], consts["neginf8"])
    epsc = consts["epsc"]

    with ExitStack() as ctx:
        E = ctx.enter_context
        xlf = E(nc.sbuf_tensor(f"xlf{tag}", [128, 16 * TH], BF16))
        oac = E(nc.sbuf_tensor(f"oac{tag}", [128, 16 * TH], BF16))
        hta = E(nc.sbuf_tensor(f"hta{tag}", [128, 16 * TH], BF16))
        sbgu = [E(nc.sbuf_tensor(f"sbgu{tag}{i}", [128, 4096], BF16)) for i in range(2)]
        sbd = E(nc.sbuf_tensor(f"sbd{tag}", [128, 32768], BF16))
        sgt = E(nc.sbuf_tensor(f"sg{tag}", [128, TH], BF16))
        wsb = E(nc.sbuf_tensor(f"wsb{tag}", [128, TH], BF16))
        W8 = E(nc.sbuf_tensor(f"W8{tag}", [8, TH], BF16))
        wtm = E(nc.sbuf_tensor(f"wtm{tag}", [128, 64], F32))
        lgsb = E(nc.sbuf_tensor(f"lgsb{tag}", [128, 64], F32))
        xch = E(nc.sbuf_tensor(f"xch{tag}", [128, 2048], F32))
        m1t = E(nc.sbuf_tensor(f"m1t{tag}", [128, 1], F32))
        negm1 = E(nc.sbuf_tensor(f"negm1{tag}", [128, 1], F32))
        m2t = E(nc.sbuf_tensor(f"m2t{tag}", [128, 1], F32))
        d1t = E(nc.sbuf_tensor(f"d1t{tag}", [128, 1], F32))
        dent = E(nc.sbuf_tensor(f"dent{tag}", [128, 1], F32))
        rden = E(nc.sbuf_tensor(f"rden{tag}", [128, 1], F32))
        mask1 = E(nc.sbuf_tensor(f"mask1{tag}", [128, 8], U8))
        lm8 = E(nc.sbuf_tensor(f"lm8{tag}", [128, 8], F32))
        keep8 = E(nc.sbuf_tensor(f"keep8{tag}", [128, 8], F32))
        x18 = E(nc.sbuf_tensor(f"x18{tag}", [128, 8], F32))
        t8 = E(nc.sbuf_tensor(f"t8{tag}", [128, 8], F32))
        ln_tiles = {n: E(nc.sbuf_tensor(f"{n}{tag}", [128, 1], F32))
                    for n in ["ssum", "qsum", "negmu", "ex2", "musq", "var",
                              "rstd_s", "rstd"]}
        sq = E(nc.sbuf_tensor(f"sq{tag}", [128, 2048], BF16))
        xlnst = E(nc.sbuf_tensor(f"xlnst{tag}", [128, 2048], BF16))
        ot_tm = E(nc.sbuf_tensor(f"ottm{tag}", [128, 2048], BF16))
        PG = E(nc.psum_tensor(f"PG{tag}", [128, 1024], F32))
        PU = E(nc.psum_tensor(f"PU{tag}", [128, 1024], F32))
        PO = [E(nc.psum_tensor(f"PO{tag}{i}", [128, 512], F32)) for i in range(2)]
        PW = E(nc.psum_tensor(f"PW{tag}", [128, 1024], F32))
        sems = dram["sems"]
        s_ld = sems["f_ld"]
        s_xw = sems["f_xw"]
        s_xt = sems["f_xt"]
        s_wgW, s_wdW = sems["f_wg"], sems["f_wd"]
        s_wgR, s_wdR = sems["f_wgr"], sems["f_wdr"]
        s_cwg, s_cwd = sems["f_cwg"], sems["f_cwd"]
        s_oa = sems["f_oa"]
        s_ot = sems["f_ot"]
        s_st = sems["f_st"]
        c_act = sems["f_act"]
        c_dve = sems["f_dve"]
        c_pe = sems["f_pe"]

        block = E(nc.Block())

        ld_rec, apply_rec, red_rec, xw_rec = {}, {}, {}, {}
        xw_all = {}
        xt_all = {}
        resid_done = {}
        st_rec = {}

        def emit_ln(half, only_chunk=None, collect=None):
            t0 = half * TH
            chunks = range(8) if only_chunk is None else [only_chunk]
            for tc8 in chunks:
                tc = (half, tc8)
                stages = []

                def _s1(half=half, tc8=tc8, tc=tc, t0=t0):
                    @block.sync
                    def _(sy):
                        if tc8 >= 1:
                            sy.wait_ge(c_dve.h, apply_rec[(half, tc8 - 1)])
                        i = sy.dma_start(xch[:, :],
                                         src_ap[t0 + tc8 * 128: t0 + (tc8 + 1) * 128, :])
                        ld_rec[tc] = s_ld.inc(i, 16)
                stages.append(_s1)

                def _s2(half=half, tc8=tc8, tc=tc):
                    @block.scalar
                    def _(sc):
                        sc.wait_ge(s_ld.h, ld_rec[tc])
                        if tc8 >= 1:
                            sc.wait_ge(c_dve.h, red_rec[(half, tc8 - 1)])
                        i = sc.activation(sq[:, :], xch[:, :], AF.Square)
                        c_act.inc(i)
                        sq_done_box[0] = c_act.n
                stages.append(_s2)
                sq_done_box = [None]

                def _s3(half=half, tc8=tc8, tc=tc):
                    sq_done = sq_done_box[0]

                    @block.vector
                    def _(v):
                        v.wait_ge(c_act.h, sq_done)
                        if tc8 >= 1:
                            v.wait_ge(c_dve.h, apply_rec[(half, tc8 - 1)])
                        i = v.reduce_sum(ln_tiles["ssum"][:, :], xch[:, :], AX.X)
                        c_dve.step(v, i)
                        i = v.reduce_sum(ln_tiles["qsum"][:, :], sq[:, :], AX.X)
                        red_rec[tc] = c_dve.step(v, i)
                        i = v.tensor_scalar(ln_tiles["negmu"][:, :], ln_tiles["ssum"][:, :],
                                            -1.0 / D, None, ALU.mult)
                        c_dve.step(v, i)
                        i = v.tensor_scalar(ln_tiles["ex2"][:, :], ln_tiles["qsum"][:, :],
                                            1.0 / D, None, ALU.mult)
                        c_dve.step(v, i)
                        i = v.tensor_tensor(ln_tiles["musq"][:, :], ln_tiles["negmu"][:, :],
                                            ln_tiles["negmu"][:, :], ALU.mult)
                        c_dve.step(v, i)
                        i = v.tensor_tensor(ln_tiles["var"][:, :], ln_tiles["ex2"][:, :],
                                            ln_tiles["musq"][:, :], ALU.subtract)
                        c_dve.inc(i)
                        var_done_box[0] = c_dve.n
                stages.append(_s3)
                var_done_box = [None]

                def _s4(tc=tc):
                    var_done = var_done_box[0]

                    @block.scalar
                    def _(sc):
                        sc.wait_ge(c_dve.h, var_done)
                        i = sc.activation(ln_tiles["rstd_s"][:, :], ln_tiles["var"][:, :],
                                          AF.Sqrt, bias=epsc[:, :])
                        c_act.inc(i)
                        sqrt_done_box[0] = c_act.n
                stages.append(_s4)
                sqrt_done_box = [None]

                def _s5(half=half, tc8=tc8, tc=tc):
                    sqrt_done = sqrt_done_box[0]

                    @block.vector
                    def _(v):
                        v.wait_ge(c_act.h, sqrt_done)
                        i = v.reciprocal(ln_tiles["rstd"][:, :], ln_tiles["rstd_s"][:, :])
                        c_dve.step(v, i)
                        prev = (half, tc8 - 1) if tc8 >= 1 else (half - 1, 7)
                        if prev in xw_rec:
                            v.wait_ge(s_xw.h, xw_rec[prev])
                        i = v.tensor_scalar(xlnst[:, :], xch[:, :],
                                            ln_tiles["negmu"][:, :], ln_tiles["rstd"][:, :],
                                            ALU.add, ALU.mult)
                        apply_rec[tc] = c_dve.inc(i)
                stages.append(_s5)

                def _s6(half=half, tc8=tc8, tc=tc, t0=t0):
                    @block.sync
                    def _(sy):
                        sy.wait_ge(c_dve.h, apply_rec[tc])
                        i = sy.dma_start(xln_d.ap()[t0 + tc8 * 128: t0 + (tc8 + 1) * 128, :],
                                         xlnst[:, :])
                        xw_rec[tc] = s_xw.inc(i, 16)
                    if tc8 == 7:
                        xw_all[half] = s_xw.n
                stages.append(_s6)

                if collect is None:
                    for st in stages:
                        st()
                else:
                    collect.extend(stages)

        def emit_xt(half, wait_pe=None):
            t0 = half * TH
            for dc in range(16):
                @block.sync
                def _(sy, half=half, dc=dc, t0=t0, wait_pe=wait_pe):
                    if dc == 0:
                        sy.wait_ge(s_xw.h, xw_all[half])
                        if wait_pe is not None:
                            sy.wait_ge(c_pe.h, wait_pe)
                    i = sy.dma_start(xlf[:, dc * TH:(dc + 1) * TH],
                                     xln_d.ap()[t0:t0 + TH, dc * 128:(dc + 1) * 128],
                                     transpose=True)
                    s_xt.inc(i, 16)
            xt_all[half] = s_xt.n

        lg_done, lg_evac, w_done = {}, {}, {}
        tr_done, tr_evac = {}, {}
        w8_all = {}

        def emit_router(half):
            for ts_ in range(8):
                reg = PW[:, 0:8] if ts_ % 2 == 0 else PW[:, 512:520]
                key = (half, ts_)

                @block.tensor
                def _(t, half=half, ts_=ts_, reg=reg, key=key):
                    if ts_ == 0:
                        t.wait_ge(s_xt.h, xt_all[half])
                        if half == 1:
                            t.wait_ge(c_dve.h, wbc_evac_rec[7])
                    if ts_ >= 2:
                        t.wait_ge(c_dve.h, lg_evac[(half, ts_ - 2)])
                    for dc in range(16):
                        i = t.matmul(reg, xlf[:, dc * TH + ts_ * 128: dc * TH + (ts_ + 1) * 128],
                                     wrb[:, dc * 8:(dc + 1) * 8],
                                     start=(dc == 0), stop=(dc == 15))
                    lg_done[key] = c_pe.inc(i)

                @block.vector
                def _(v, key=key, reg=reg, half=half, ts_=ts_):
                    v.wait_ge(c_pe.h, lg_done[key])
                    if half == 1 and ts_ == 0:
                        v.wait_ge(c_dve.h, ht_rec[8 * 16 - 1])
                    i = v.tensor_copy(lgsb[:, ts_ * 8:(ts_ + 1) * 8], reg)
                    lg_evac[key] = c_dve.inc(i)

            for ts_ in range(8):
                key = (half, ts_)

                @block.vector
                def _(v, half=half, ts_=ts_, key=key):
                    Ls = lgsb[:, ts_ * 8:(ts_ + 1) * 8]
                    v.wait_ge(c_dve.h, lg_evac[key])
                    prev = (half, ts_ - 1) if ts_ >= 1 else (half - 1, 7)
                    if prev in w_done:
                        v.wait_ge(c_dve.h, w_done[prev])
                    i = v.reduce_max(m1t[:, :], Ls, AX.X)
                    c_dve.step(v, i)
                    i = v.tensor_scalar(negm1[:, :], m1t[:, :], -1.0, None, ALU.mult)
                    c_dve.step(v, i)
                    i = v.tensor_scalar(mask1[:, :], Ls, m1t[:, :], None, ALU.is_ge)
                    c_dve.step(v, i)
                    i = v.select(lm8[:, :], mask1[:, :], neginf8[:, :], Ls, add_drain=True)
                    c_dve.step(v, i)
                    i = v.reduce_max(m2t[:, :], lm8[:, :], AX.X)
                    c_dve.inc(i)

                mx_done = c_dve.n

                @block.scalar
                def _(sc, ts_=ts_, mx_done=mx_done):
                    sc.wait_ge(c_dve.h, mx_done)
                    sc.activation(x18[:, :], lgsb[:, ts_ * 8:(ts_ + 1) * 8], AF.Exp,
                                  bias=negm1[:, :])
                    i = sc.activation(d1t[:, :], m2t[:, :], AF.Exp, bias=negm1[:, :])
                    c_act.inc(i)

                exp_done = c_act.n

                @block.vector
                def _(v, ts_=ts_, key=key, exp_done=exp_done):
                    v.wait_ge(c_act.h, exp_done)
                    i = v.tensor_scalar(dent[:, :], d1t[:, :], 1.0, None, ALU.add)
                    c_dve.step(v, i)
                    i = v.reciprocal(rden[:, :], dent[:, :])
                    c_dve.step(v, i)
                    i = v.tensor_scalar(keep8[:, :], lgsb[:, ts_ * 8:(ts_ + 1) * 8],
                                        m2t[:, :], None, ALU.is_ge)
                    c_dve.step(v, i)
                    i = v.tensor_tensor(t8[:, :], x18[:, :], keep8[:, :], ALU.mult)
                    c_dve.step(v, i)
                    i = v.tensor_scalar(wtm[:, ts_ * 8:(ts_ + 1) * 8], t8[:, :],
                                        rden[:, :], None, ALU.mult)
                    w_done[key] = c_dve.inc(i)

            for ts_ in range(8):
                reg = PW[0:8, 0:128] if ts_ % 2 == 0 else PW[0:8, 512:640]
                key = (half, ts_)

                @block.tensor
                def _(t, ts_=ts_, reg=reg, key=key, half=half):
                    t.wait_ge(c_dve.h, w_done[key])
                    if ts_ >= 2:
                        t.wait_ge(c_dve.h, tr_evac[(half, ts_ - 2)])
                    i = t.transpose(reg, wtm[:, ts_ * 8:(ts_ + 1) * 8], identf32[:, :])
                    tr_done[key] = c_pe.inc(i)

                @block.vector
                def _(v, ts_=ts_, reg=reg, key=key):
                    v.wait_ge(c_pe.h, tr_done[key])
                    i = v.tensor_copy(W8[0:8, ts_ * 128:(ts_ + 1) * 128], reg)
                    tr_evac[key] = c_dve.inc(i)
            w8_all[half] = c_dve.n

        g_rec, u_rec = {}, {}
        wg_rec, wu_rec, wd_rec = {}, {}, {}
        cwg_rec, cwu_rec, cwd_rec = {}, {}, {}
        wd_used_rec = {}
        emitted_loads = set()

        def emit_wload(eh, fc):
            if (eh, fc) in emitted_loads:
                return
            emitted_loads.add((eh, fc))
            half = eh // 8
            e = eh % 8
            k = eh * 16 + fc
            wmode = wmodes[half]
            if wmode == 'write':
                @block.gpsimd
                def _(g, e=e, fc=fc, k=k):
                    if k >= 2:
                        g.wait_ge(c_pe.h, u_rec[k - 2])
                        g.wait_ge(s_cwg[fc % 2].h, cwg_rec[k - 2])
                    i = g.dma_start(sbgu[fc % 2][:, :], wg_d.ap()[e, fc])
                    wg_rec[k] = s_wgW[fc % 2].inc(i, 16)

                @block.sync
                def _(sy, e=e, fc=fc, k=k):
                    sy.wait_ge(s_wgW[fc % 2].h, wg_rec[k])
                    i = sy.dma_start(wgc_d.ap()[e, fc], sbgu[fc % 2][:, :])
                    cwg_rec[k] = s_cwg[fc % 2].inc(i, 16)
            else:
                @block.sync
                def _(sy, e=e, fc=fc, k=k):
                    if k >= 2:
                        sy.wait_ge(c_pe.h, u_rec[k - 2])
                        if wmodes[(k - 2) // 128] == 'write':
                            sy.wait_ge(s_cwg[fc % 2].h, cwg_rec[k - 2])
                    i = sy.dma_start(sbgu[fc % 2][:, :], wgc_d.ap()[e, fc])
                    wg_rec[k] = s_wgR[fc % 2].inc(i, 16)

        def emit_wd_load(eh):
            if ("wd", eh) in emitted_loads:
                return
            emitted_loads.add(("wd", eh))
            half = eh // 8
            e = eh % 8
            wmode = wmodes[half]
            if wmode == 'write':
                @block.gpsimd
                def _(g, e=e, eh=eh):
                    if eh >= 1:
                        g.wait_ge(c_pe.h, wd_used_rec[eh - 1])
                        g.wait_ge(s_cwd.h, cwd_rec[eh - 1])
                    for q in range(4):
                        i = g.dma_start(sbd[:, q * 8192:(q + 1) * 8192],
                                        wd_d.ap()[e][:, q * 4:(q + 1) * 4, :]
                                        .rearrange("p fk do -> p (fk do)"))
                        s_wdW.inc(i, 16)
                    wd_rec[eh] = s_wdW.n

                @block.sync
                def _(sy, e=e, eh=eh):
                    sy.wait_ge(s_wdW.h, wd_rec[eh])
                    for q in range(4):
                        i = sy.dma_start(wdc_d.ap()[e, :, q * 8192:(q + 1) * 8192],
                                         sbd[:, q * 8192:(q + 1) * 8192])
                        s_cwd.inc(i, 16)
                    cwd_rec[eh] = s_cwd.n
            else:
                @block.sync
                def _(sy, e=e, eh=eh):
                    if eh >= 1:
                        sy.wait_ge(c_pe.h, wd_used_rec[eh - 1])
                        if wmodes[(eh - 1) // 8] == 'write':
                            sy.wait_ge(s_cwd.h, cwd_rec[eh - 1])
                    for q in range(4):
                        i = sy.dma_start(sbd[:, q * 8192:(q + 1) * 8192],
                                         wdc_d.ap()[e, :, q * 8192:(q + 1) * 8192])
                        s_wdR.inc(i, 16)
                    wd_rec[eh] = s_wdR.n

        def wd_sem(eh):
            return s_wdW if wmodes[eh // 8] == 'write' else s_wdR

        def wg_sem(eh, fc):
            return (s_wgW if wmodes[eh // 8] == 'write' else s_wgR)[fc % 2]

        sil_rec, h1_rec, ht_rec = {}, {}, {}
        wbc_rec, wbc_evac_rec = {}, {}
        po_evac_rec = {}
        oa_ready = {}

        def emit_expert(eh):
            half = eh // 8
            e = eh % 8
            emit_wload(eh, 0)
            emit_wload(eh, 1)

            @block.tensor
            def _(t, eh=eh, half=half):
                if eh == 0:
                    t.wait_ge(c_dve.h, w8_all[0])
                elif eh == 8:
                    t.wait_ge(c_dve.h, w8_all[1])
                    t.wait_ge(c_dve.h, wbc_evac_rec[eh - 1])
                else:
                    t.wait_ge(c_dve.h, wbc_evac_rec[eh - 1])
                for th in range(2):
                    i = t.matmul(PW[:, th * 512:(th + 1) * 512],
                                 self32[0:8, (eh % 8) * 128:((eh % 8) + 1) * 128],
                                 W8[0:8, th * 512:(th + 1) * 512],
                                 start=True, stop=True)
                wbc_rec[eh] = c_pe.inc(i)

            @block.vector
            def _(v, eh=eh):
                v.wait_ge(c_pe.h, wbc_rec[eh])
                if eh >= 1:
                    v.wait_ge(c_dve.h, ht_rec[eh * 16 - 1])
                i = v.tensor_scalar(wsb[:, :], PW[:, 0:1024], 0.5, None, ALU.mult)
                wbc_evac_rec[eh] = c_dve.inc(i)

            for fc in range(16):
                emit_wload(eh, fc)
                if fc == 4:
                    emit_wd_load(eh)
                if ln_stage_q:
                    ln_stage_q.pop(0)()
                k = eh * 16 + fc

                @block.tensor
                def _(t, eh=eh, half=half, fc=fc, k=k):
                    t.wait_ge(wg_sem(eh, fc).h, wg_rec[k])
                    if k >= 1:
                        t.wait_ge(c_act.h, sil_rec[k - 1])
                    for dc in range(16):
                        for tt in range(2):
                            i = t.matmul(PG[:, tt * 512:(tt + 1) * 512],
                                         sbgu[fc % 2][:, dc * 128:(dc + 1) * 128],
                                         xlf[:, dc * TH + tt * 512: dc * TH + (tt + 1) * 512],
                                         start=(dc == 0), stop=(dc == 15))
                    g_rec[k] = c_pe.inc(i)
                    if k >= 1:
                        t.wait_ge(c_dve.h, h1_rec[k - 1])
                    for dc in range(16):
                        for tt in range(2):
                            i = t.matmul(PU[:, tt * 512:(tt + 1) * 512],
                                         sbgu[fc % 2][:, 2048 + dc * 128: 2048 + (dc + 1) * 128],
                                         xlf[:, dc * TH + tt * 512: dc * TH + (tt + 1) * 512],
                                         start=(dc == 0), stop=(dc == 15))
                    u_rec[k] = c_pe.inc(i)

                @block.scalar
                def _(sc, k=k):
                    sc.wait_ge(c_pe.h, g_rec[k])
                    if k >= 1:
                        sc.wait_ge(c_dve.h, h1_rec[k - 1])
                    i = sc.activation(sgt[:, :], PG[:, 0:1024], AF.Silu)
                    sil_rec[k] = c_act.inc(i)

                @block.vector
                def _(v, k=k, fc=fc):
                    v.wait_ge(c_act.h, sil_rec[k])
                    v.wait_ge(c_pe.h, u_rec[k])
                    if k >= 1:
                        v.wait_ge(c_dve.h, ht_rec[k - 1])
                    hsl = hta[:, fc * TH:(fc + 1) * TH]
                    i = v.tensor_tensor(hsl, sgt[:, :], PU[:, 0:1024], ALU.mult)
                    h1_rec[k] = c_dve.step(v, i)
                    i = v.tensor_tensor(hsl, hsl, wsb[:, :], ALU.mult)
                    ht_rec[k] = c_dve.inc(i)

            last_ht = ht_rec[eh * 16 + 15]
            for dco in range(16):
                for tt in range(2):
                    kk = eh * 32 + dco * 2 + tt

                    @block.tensor
                    def _(t, eh=eh, dco=dco, tt=tt, kk=kk, last_ht=last_ht):
                        if dco == 0 and tt == 0:
                            t.wait_ge(wd_sem(eh).h, wd_rec[eh])
                            t.wait_ge(c_dve.h, last_ht)
                        if kk >= 2:
                            t.wait_ge(c_dve.h, po_evac_rec[kk - 2])
                        for fk in range(16):
                            i = t.matmul(PO[kk % 2][:, :],
                                         sbd[:, fk * 2048 + dco * 128: fk * 2048 + (dco + 1) * 128],
                                         hta[:, fk * TH + tt * 512: fk * TH + (tt + 1) * 512],
                                         start=(fk == 0), stop=(fk == 15))
                        c_pe.inc(i)

                    po_done = c_pe.n

                    @block.vector
                    def _(v, eh=eh, dco=dco, tt=tt, kk=kk, po_done=po_done):
                        v.wait_ge(c_pe.h, po_done)
                        e8 = eh % 8
                        if e8 >= 1:
                            v.wait_ge(c_dve.h, po_evac_rec[kk - 32])
                        elif eh == 8:
                            v.wait_ge(s_oa.h, oa_ready[0])
                        dsl = oac[:, dco * TH + tt * 512: dco * TH + (tt + 1) * 512]
                        if e8 == 0:
                            i = v.tensor_copy(dsl, PO[kk % 2][:, :])
                        else:
                            i = v.tensor_tensor(dsl, dsl, PO[kk % 2][:, :], ALU.add)
                        po_evac_rec[kk] = c_dve.inc(i)

            wd_used_rec[eh] = c_pe.n

        def emit_oa_out(half):
            oac_done = c_dve.n
            t0 = half * TH
            for g4 in range(4):
                @block.sync
                def _(sy, g4=g4, t0=t0, oac_done=oac_done):
                    if g4 == 0:
                        sy.wait_ge(c_dve.h, oac_done)
                    i = sy.dma_start(
                        oacc_d.ap()[:, t0:t0 + TH]
                        .rearrange("(dc p) t -> p dc t", p=128)[:, g4 * 4:(g4 + 1) * 4, :],
                        oac[:, :].rearrange("p (dc t) -> p dc t", t=TH)[:, g4 * 4:(g4 + 1) * 4, :])
                    s_oa.inc(i, 16)
            oa_ready[half] = s_oa.n

        def emit_resid(half, queue="sync"):
            t0 = half * TH
            ot_rec, res_ld_rec, xnew_rec = {}, {}, {}
            dec = block.sync
            for tc8 in range(8):
                @dec
                def _(sy, half=half, tc8=tc8, t0=t0):
                    if tc8 == 0:
                        sy.wait_ge(s_oa.h, oa_ready[half])
                        if half == 0:
                            sy.wait_ge(c_dve.h, apply_rec[(1, 7)])
                        else:
                            sy.wait_ge(s_st.h, st_rec[(0, 7)])
                    if tc8 >= 1:
                        sy.wait_ge(c_dve.h, xnew_rec[tc8 - 1])
                    i = sy.dma_start(ot_tm[:, :],
                                     oacc_d.ap()[:, t0 + tc8 * 128: t0 + (tc8 + 1) * 128],
                                     transpose=True)
                    ot_rec[tc8] = s_ot.inc(i, 16)
                    if tc8 >= 1:
                        sy.wait_ge(s_st.h, st_rec[(half, tc8 - 1)])
                    i = sy.dma_start(xch[:, :],
                                     src_ap[t0 + tc8 * 128: t0 + (tc8 + 1) * 128, :])
                    res_ld_rec[tc8] = s_ld.inc(i, 16)

                @block.vector
                def _(v, half=half, tc8=tc8):
                    v.wait_ge(s_ot.h, ot_rec[tc8])
                    v.wait_ge(s_ld.h, res_ld_rec[tc8])
                    i = v.tensor_tensor(xch[:, :], xch[:, :],
                                        ot_tm[:, :], ALU.add)
                    xnew_rec[tc8] = c_dve.inc(i)

                @dec
                def _(sy, half=half, tc8=tc8, t0=t0):
                    sy.wait_ge(c_dve.h, xnew_rec[tc8])
                    i = sy.dma_start(dst_ap[t0 + tc8 * 128: t0 + (tc8 + 1) * 128, :],
                                     xch[:, :])
                    st_rec[(half, tc8)] = s_st.inc(i, 16)
            resid_done[half] = c_dve.n

        ln_stage_q = []
        emit_ln(0)
        emit_xt(0)
        emit_router(0)
        emit_ln(1, collect=ln_stage_q)
        for eh in range(8):
            emit_expert(eh)
        while ln_stage_q:
            ln_stage_q.pop(0)()
        emit_xt(1, wait_pe=u_rec[8 * 16 - 1])
        emit_wload(8, 0)
        emit_wload(8, 1)
        emit_oa_out(0)
        emit_router(1)
        emit_wd_load(8)
        emit_resid(0)
        for eh in range(8, 16):
            emit_expert(eh)
        emit_oa_out(1)
        emit_resid(1)

        @block.sync
        def _(sy):
            sy.wait_ge(s_st.h, s_st.n)
            if 'write' in wmodes:
                sy.wait_ge(s_cwg[0].h, s_cwg[0].n)
                sy.wait_ge(s_cwg[1].h, s_cwg[1].n)
                sy.wait_ge(s_cwd.h, s_cwd.n)


def attn_sublayer(nc, consts, dram, src_ap, dst_ap):
    xln_d, oacc_d = dram["xln"], dram["oacc"]
    qk_d, od_d = dram["qk"], dram["od"]
    wqt_d, wot_d = dram["wqt"], dram["wot"]
    identbf, masktri, epsc = consts["identbf"], consts["masktri"], consts["epsc"]
    bqc, bqs, boc = consts["bqc"], consts["bqs"], consts["boc"]
    bqr_d = consts["bqr_d"]

    with ExitStack() as ctxS:
        ES = ctxS.enter_context
        vtm = ES(nc.sbuf_tensor("vtm", [128, 32768], BF16))
        with ExitStack() as ctx:
            E = ctx.enter_context
            xlf = E(nc.sbuf_tensor("xlfA", [128, 32768], BF16))
            qsl = [E(nc.sbuf_tensor(f"qslA{i}", [128, 4096], BF16)) for i in range(2)]
            qkst = [E(nc.sbuf_tensor(f"qkstA{i}", [128, 2048], BF16)) for i in range(2)]
            xch = E(nc.sbuf_tensor("xchA", [128, 2048], F32))
            ln_tiles = {n: E(nc.sbuf_tensor(f"{n}A", [128, 1], F32))
                        for n in ["ssum", "qsum", "negmu", "ex2", "musq", "var",
                                  "rstd_s", "rstd"]}
            sq = E(nc.sbuf_tensor("sqA", [128, 2048], BF16))
            xlnst = E(nc.sbuf_tensor("xlnstA", [128, 2048], BF16))
            bqrv = E(nc.sbuf_tensor("bqrv", [1, 2048], BF16))
            ones1 = E(nc.sbuf_tensor("ones1", [1, 128], BF16))
            PQ = E(nc.psum_tensor("PQA", [128, 2048], F32))
            PV = [E(nc.psum_tensor(f"PVA{i}", [128, 512], F32)) for i in range(2)]
            sems = dram["sems"]
            s_ld = sems["a_ld"]
            s_xw = sems["a_xw"]
            s_xt = sems["a_xt"]
            s_qs = sems["a_qs"]
            s_qw = sems["a_qw"]
            c_act = sems["a_act"]
            c_dve = sems["a_dve"]
            c_pe = sems["a_pe"]
            s_bv = sems["a_bv"]
            block = E(nc.Block())

            @block.gpsimd
            def _(g):
                i = g.dma_start(bqrv[:, :], bqr_d.ap()[0:1, 4096:6144])
                s_bv.inc(i, 16)
                g.memset(ones1[:, :], 1.0)
                g.wait_ge(s_bv.h, s_bv.n)

            ld_rec, apply_rec, red_rec, xw_rec = {}, {}, {}, {}
            for tc in range(16):
                @block.sync
                def _(sy, tc=tc):
                    if tc >= 1:
                        sy.wait_ge(c_dve.h, apply_rec[tc - 1])
                    i = sy.dma_start(xch[:, :],
                                     src_ap[tc * 128:(tc + 1) * 128, :])
                    ld_rec[tc] = s_ld.inc(i, 16)

                @block.scalar
                def _(sc, tc=tc):
                    sc.wait_ge(s_ld.h, ld_rec[tc])
                    if tc >= 1:
                        sc.wait_ge(c_dve.h, red_rec[tc - 1])
                    i = sc.activation(sq[:, :], xch[:, :], AF.Square)
                    c_act.inc(i)

                sq_done = c_act.n

                @block.vector
                def _(v, tc=tc, sq_done=sq_done):
                    v.wait_ge(c_act.h, sq_done)
                    if tc >= 1:
                        v.wait_ge(c_dve.h, apply_rec[tc - 1])
                    i = v.reduce_sum(ln_tiles["ssum"][:, :], xch[:, :], AX.X)
                    c_dve.step(v, i)
                    i = v.reduce_sum(ln_tiles["qsum"][:, :], sq[:, :], AX.X)
                    red_rec[tc] = c_dve.step(v, i)
                    i = v.tensor_scalar(ln_tiles["negmu"][:, :], ln_tiles["ssum"][:, :],
                                        -1.0 / D, None, ALU.mult)
                    c_dve.step(v, i)
                    i = v.tensor_scalar(ln_tiles["ex2"][:, :], ln_tiles["qsum"][:, :],
                                        1.0 / D, None, ALU.mult)
                    c_dve.step(v, i)
                    i = v.tensor_tensor(ln_tiles["musq"][:, :], ln_tiles["negmu"][:, :],
                                        ln_tiles["negmu"][:, :], ALU.mult)
                    c_dve.step(v, i)
                    i = v.tensor_tensor(ln_tiles["var"][:, :], ln_tiles["ex2"][:, :],
                                        ln_tiles["musq"][:, :], ALU.subtract)
                    c_dve.inc(i)

                var_done = c_dve.n

                @block.scalar
                def _(sc, tc=tc, var_done=var_done):
                    sc.wait_ge(c_dve.h, var_done)
                    i = sc.activation(ln_tiles["rstd_s"][:, :], ln_tiles["var"][:, :],
                                      AF.Sqrt, bias=epsc[:, :])
                    c_act.inc(i)

                sqrt_done = c_act.n

                @block.vector
                def _(v, tc=tc, sqrt_done=sqrt_done):
                    v.wait_ge(c_act.h, sqrt_done)
                    i = v.reciprocal(ln_tiles["rstd"][:, :], ln_tiles["rstd_s"][:, :])
                    c_dve.step(v, i)
                    if tc >= 1:
                        v.wait_ge(s_xw.h, xw_rec[tc - 1])
                    i = v.tensor_scalar(xlnst[:, :], xch[:, :],
                                        ln_tiles["negmu"][:, :], ln_tiles["rstd"][:, :],
                                        ALU.add, ALU.mult)
                    apply_rec[tc] = c_dve.inc(i)

                @block.sync
                def _(sy, tc=tc):
                    sy.wait_ge(c_dve.h, apply_rec[tc])
                    i = sy.dma_start(xln_d.ap()[tc * 128:(tc + 1) * 128, :],
                                     xlnst[:, :])
                    xw_rec[tc] = s_xw.inc(i, 16)

            xw_all = s_xw.n
            for dc in range(16):
                @block.sync
                def _(sy, dc=dc):
                    if dc == 0:
                        sy.wait_ge(s_xw.h, xw_all)
                    i = sy.dma_start(xlf[:, dc * 2048:(dc + 1) * 2048],
                                     xln_d.ap()[0:S, dc * 128:(dc + 1) * 128],
                                     transpose=True)
                    s_xt.inc(i, 16)
            xt_all = s_xt.n

            qs_rec = {}
            pe_slab_done = {}

            def load_slab(idx, eb):
                @block.gpsimd
                def _(g, idx=idx, eb=eb):
                    if idx >= 2:
                        g.wait_ge(c_pe.h, pe_slab_done[idx - 2])
                    i = g.dma_start(qsl[idx % 2][:, :],
                                    wqt_d.ap()[eb].rearrange("p dc e -> p (dc e)"))
                    qs_rec[idx] = s_qs[idx % 2].inc(i, 16)

            vevac_rec = {}
            for ebv in range(8):
                load_slab(ebv, 16 + ebv)
                for tc in range(16):
                    kk = ebv * 16 + tc

                    @block.tensor
                    def _(t, ebv=ebv, tc=tc, kk=kk):
                        if tc == 0:
                            t.wait_ge(s_qs[ebv % 2].h, qs_rec[ebv])
                            if ebv == 0:
                                t.wait_ge(s_xt.h, xt_all)
                        if kk >= 2:
                            t.wait_ge(c_dve.h, vevac_rec[kk - 2])
                        for dc in range(16):
                            t.matmul(PV[kk % 2][:, 0:256],
                                     xlf[:, dc * 2048 + tc * 128: dc * 2048 + (tc + 1) * 128],
                                     qsl[ebv % 2][:, dc * 256:(dc + 1) * 256],
                                     start=(dc == 0), stop=False)
                        i = t.matmul(PV[kk % 2][:, 0:256], ones1[0:1, :],
                                     bqrv[0:1, ebv * 256:(ebv + 1) * 256],
                                     start=False, stop=True)
                        c_pe.inc(i)
                        if tc == 15:
                            pe_slab_done[ebv] = c_pe.n

                    pv_done = c_pe.n

                    @block.vector
                    def _(v, ebv=ebv, tc=tc, kk=kk, pv_done=pv_done):
                        v.wait_ge(c_pe.h, pv_done)
                        i = v.tensor_copy(
                            vtm[:, tc * 2048 + ebv * 256: tc * 2048 + (ebv + 1) * 256],
                            PV[kk % 2][:, 0:256])
                        vevac_rec[kk] = c_dve.inc(i)

            qk_evac_rec = {}
            qw_rec = {}
            for ebq in range(16):
                sidx = 8 + ebq
                load_slab(sidx, ebq)
                for ec2 in range(2):
                    ec = ebq * 2 + ec2
                    gi = ebq * 2 + ec2

                    @block.tensor
                    def _(t, ebq=ebq, ec2=ec2, sidx=sidx, gi=gi):
                        if ec2 == 0:
                            t.wait_ge(s_qs[sidx % 2].h, qs_rec[sidx])
                        if gi >= 1:
                            t.wait_ge(c_dve.h, qk_evac_rec[gi - 1])
                        for tt in range(4):
                            for dc in range(16):
                                i = t.matmul(
                                    PQ[:, tt * 512:(tt + 1) * 512],
                                    qsl[sidx % 2][:, dc * 256 + ec2 * 128: dc * 256 + ec2 * 128 + 128],
                                    xlf[:, dc * 2048 + tt * 512: dc * 2048 + (tt + 1) * 512],
                                    start=(dc == 0), stop=(dc == 15))
                        c_pe.inc(i)
                        if ec2 == 1:
                            pe_slab_done[sidx] = c_pe.n

                    pq_done = c_pe.n

                    @block.vector
                    def _(v, ec=ec, gi=gi, pq_done=pq_done):
                        v.wait_ge(c_pe.h, pq_done)
                        if gi >= 2:
                            v.wait_ge(s_qw[gi % 2].h, qw_rec[gi - 2])
                        if ec < 16:
                            bias, scale = bqs[:, ec:ec + 1], ISQ
                        else:
                            bias, scale = bqc[:, ec:ec + 1], 1.0
                        for tt in range(4):
                            i = v.tensor_scalar(qkst[gi % 2][:, tt * 512:(tt + 1) * 512],
                                                PQ[:, tt * 512:(tt + 1) * 512],
                                                bias, scale, ALU.add, ALU.mult)
                        qk_evac_rec[gi] = c_dve.inc(i)

                    @block.sync
                    def _(sy, ec=ec, gi=gi):
                        sy.wait_ge(c_dve.h, qk_evac_rec[gi])
                        i = sy.dma_start(qk_d.ap()[ec * 128:(ec + 1) * 128, :],
                                         qkst[gi % 2][:, :])
                        qw_rec[gi] = s_qw[gi % 2].inc(i, 16)

            @block.sync
            def _(sy):
                sy.wait_ge(s_qw[0].h, s_qw[0].n)
                sy.wait_ge(s_qw[1].h, s_qw[1].n)

        with ExitStack() as ctx:
            E = ctx.enter_context
            osb = E(nc.sbuf_tensor("osb", [128, 32768], BF16))
            qkh = [E(nc.sbuf_tensor(f"qkh{i}", [128, 4096], BF16)) for i in range(4)]
            pbf = [E(nc.sbuf_tensor(f"pbf{i}", [128, 2048], BF16)) for i in range(4)]
            ptsb = [E(nc.sbuf_tensor(f"ptsb{i}", [128, 2048], BF16)) for i in range(4)]
            mrow = [E(nc.sbuf_tensor(f"mrow{i}", [128, 1], F32)) for i in range(4)]
            negm = [E(nc.sbuf_tensor(f"negm{i}", [128, 1], F32)) for i in range(4)]
            rs = [E(nc.sbuf_tensor(f"rs{i}", [128, 1], F32)) for i in range(4)]
            rden = [E(nc.sbuf_tensor(f"rdenA{i}", [128, 1], F32)) for i in range(4)]
            PSC = E(nc.psum_tensor("PSC", [128, 2048], F32))
            PT = [E(nc.psum_tensor(f"PT{i}", [128, 1024], BF16)) for i in range(2)]
            POT = [E(nc.psum_tensor(f"POT{i}", [128, 512], F32)) for i in range(2)]
            sems = dram["sems"]
            s_qk = sems["b_qk4"]
            s_od = sems["b_od"]
            c_act = sems["a_act"]
            c_dve = sems["a_dve"]
            c_pe = sems["a_pe"]
            block = E(nc.Block())

            qk_rec = {}
            head_pe_done = {}

            def load_head(h):
                @block.sync
                def _(sy, h=h):
                    if h >= 4:
                        sy.wait_ge(c_pe.h, head_pe_done[h - 4])
                    i = sy.dma_start(
                        qkh[h % 4][:, :].rearrange("p (w t) -> p w t", w=2),
                        qk_d.ap().rearrange("(w r) t -> r w t", w=2)[h * 128:(h + 1) * 128])
                    qk_rec[h] = s_qk[h % 4].inc(i, 16)

            for hh in range(4):
                load_head(hh)

            mask_rec, exp_rec, sum_rec = {}, {}, {}
            pt_evac_rec, oevac_rec, av_all_rec = {}, {}, {}
            sc_rec = {}
            psc_prev_exp = [None]

            def E_S(h, qc):
                kext = (qc + 1) * 128
                nkt = (kext + 511) // 512
                prev = psc_prev_exp[0]

                @block.tensor
                def _(t, h=h, qc=qc, kext=kext, nkt=nkt, prev=prev):
                    if qc == 0:
                        t.wait_ge(s_qk[h % 4].h, qk_rec[h])
                    if prev is not None:
                        t.wait_ge(c_act.h, exp_rec[prev])
                    for kt in range(nkt):
                        k0 = kt * 512
                        k1 = min(kext, k0 + 512)
                        i = t.matmul(PSC[:, k0:k1],
                                     qkh[h % 4][:, qc * 128:(qc + 1) * 128],
                                     qkh[h % 4][:, 2048 + k0: 2048 + k1],
                                     start=True, stop=True)
                    c_pe.inc(i)
                sc_rec[(h, qc)] = c_pe.n

            def E_softmax(h, qc):
                p = h % 4
                kext = (qc + 1) * 128

                @block.vector
                def _(v, h=h, qc=qc, p=p, kext=kext):
                    v.wait_ge(c_pe.h, sc_rec[(h, qc)])
                    if qc >= 1:
                        v.wait_ge(c_act.h, exp_rec[(h, qc - 1)])
                    i = v.tensor_tensor(PSC[:, kext - 128:kext],
                                        PSC[:, kext - 128:kext],
                                        masktri[:, :], ALU.add)
                    c_dve.step(v, i)
                    i = v.reduce_max(mrow[p][:, :], PSC[:, 0:kext], AX.X)
                    c_dve.step(v, i)
                    i = v.tensor_scalar(negm[p][:, :], mrow[p][:, :], -1.0, None, ALU.mult)
                    mask_rec[(h, qc)] = c_dve.inc(i)

                @block.scalar
                def _(sc, h=h, qc=qc, p=p, kext=kext):
                    sc.wait_ge(c_dve.h, mask_rec[(h, qc)])
                    if qc >= 1:
                        sc.wait_ge(c_dve.h, sum_rec[(h, qc - 1)])
                        sc.wait_ge(c_pe.h, av_all_rec[(h, qc - 1)])
                    i = sc.activation(pbf[p][:, 0:kext], PSC[:, 0:kext],
                                      AF.Exp, bias=negm[p][:, :], accum_out=rs[p][:, :])
                    exp_rec[(h, qc)] = c_act.inc(i)
                psc_prev_exp[0] = (h, qc)

                @block.vector
                def _(v, h=h, qc=qc, p=p):
                    v.wait_ge(c_act.h, exp_rec[(h, qc)])
                    i = v.reciprocal(rden[p][:, :], rs[p][:, :])
                    sum_rec[(h, qc)] = c_dve.inc(i)

            def E_TAV(h, qc):
                p = h % 4
                sl = h % 2
                nb = (qc + 4) // 4
                pt_prev = pt_last.get(sl)
                for b in range(nb):
                    kc0, kc1 = b * 4, min(qc + 1, b * 4 + 4)

                    @block.tensor
                    def _(t, h=h, qc=qc, p=p, sl=sl, b=b, kc0=kc0, kc1=kc1, pt_prev=pt_prev):
                        if b == 0:
                            t.wait_ge(c_act.h, exp_rec[(h, qc)])
                            if pt_prev is not None:
                                t.wait_ge(c_dve.h, pt_prev)
                        else:
                            t.wait_ge(c_dve.h, pt_evac_rec[(h, qc, b - 1)])
                        for kc in range(kc0, kc1):
                            i = t.transpose(PT[sl][:, (kc % 4) * 128:(kc % 4 + 1) * 128],
                                            pbf[p][:, kc * 128:(kc + 1) * 128],
                                            identbf[:, :])
                        c_pe.inc(i)

                    tr_rec = c_pe.n

                    @block.vector
                    def _(v, h=h, qc=qc, p=p, sl=sl, b=b, kc0=kc0, kc1=kc1, tr_rec=tr_rec):
                        v.wait_ge(c_pe.h, tr_rec)
                        if qc >= 1 and b == 0:
                            v.wait_ge(c_pe.h, av_all_rec[(h, qc - 1)])
                        i = v.tensor_copy(ptsb[p][:, kc0 * 128:kc1 * 128],
                                          PT[sl][:, 0:(kc1 - kc0) * 128])
                        pt_evac_rec[(h, qc, b)] = c_dve.inc(i)
                pt_last[sl] = pt_evac_rec[(h, qc, nb - 1)]

                pot_prev = pot_last.get(sl)

                @block.tensor
                def _(t, h=h, qc=qc, p=p, sl=sl, pot_prev=pot_prev):
                    if pot_prev is not None:
                        t.wait_ge(c_dve.h, pot_prev)
                    for kc in range(qc + 1):
                        if kc % 4 == 0:
                            t.wait_ge(c_dve.h, pt_evac_rec[(h, qc, kc // 4)])
                        i = t.matmul(POT[sl][:, 0:128],
                                     ptsb[p][:, kc * 128:(kc + 1) * 128],
                                     vtm[:, kc * 2048 + h * 128: kc * 2048 + (h + 1) * 128],
                                     start=(kc == 0), stop=(kc == qc))
                        c_pe.inc(i)
                    av_all_rec[(h, qc)] = c_pe.n
                    if qc == 15:
                        head_pe_done[h] = c_pe.n

                av_done = c_pe.n

                @block.vector
                def _(v, h=h, qc=qc, p=p, sl=sl, av_done=av_done):
                    v.wait_ge(c_pe.h, av_done)
                    v.wait_ge(c_dve.h, sum_rec[(h, qc)])
                    i = v.tensor_scalar(
                        osb[:, qc * 2048 + h * 128: qc * 2048 + (h + 1) * 128],
                        POT[sl][:, 0:128], rden[p][:, :], None, ALU.mult)
                    oevac_rec[(h, qc)] = c_dve.inc(i)
                pot_last[sl] = oevac_rec[(h, qc)]

            pt_last, pot_last = {}, {}
            for hg in range(NH // 4):
                hs = [4 * hg + i for i in range(4)]
                for qc in range(16):
                    E_S(hs[0], qc)
                    E_softmax(hs[0], qc)
                    if qc >= 1:
                        E_TAV(hs[2], qc - 1)
                    E_S(hs[1], qc)
                    E_softmax(hs[1], qc)
                    if qc >= 1:
                        E_TAV(hs[3], qc - 1)
                    E_S(hs[2], qc)
                    E_softmax(hs[2], qc)
                    E_TAV(hs[0], qc)
                    E_S(hs[3], qc)
                    E_softmax(hs[3], qc)
                    E_TAV(hs[1], qc)
                E_TAV(hs[2], 15)
                E_TAV(hs[3], 15)
                if hg + 1 < NH // 4:
                    for nh_ in range(4):
                        load_head(4 * (hg + 1) + nh_)

            o_all = c_dve.n
            for g4 in range(4):
                @block.sync
                def _(sy, g4=g4):
                    if g4 == 0:
                        sy.wait_ge(c_dve.h, o_all)
                    i = sy.dma_start(
                        od_d.ap().rearrange("(qc p) dv -> p qc dv", p=128)[:, g4 * 4:(g4 + 1) * 4, :],
                        osb[:, :].rearrange("p (qc dv) -> p qc dv", dv=2048)[:, g4 * 4:(g4 + 1) * 4, :])
                    s_od.inc(i, 16)

            @block.sync
            def _(sy):
                sy.wait_ge(s_od.h, s_od.n)

        with ExitStack() as ctx:
            E = ctx.enter_context
            ofm = E(nc.sbuf_tensor("ofm", [128, 32768], BF16))
            wsl = [E(nc.sbuf_tensor(f"wslC{i}", [128, 4096], BF16)) for i in range(2)]
            oat = [E(nc.sbuf_tensor(f"oatC{i}", [128, 2048], BF16)) for i in range(2)]
            xch5 = E(nc.sbuf_tensor("xch5", [128, 2048], F32))
            ot5 = [E(nc.sbuf_tensor(f"ot5{i}", [128, 2048], BF16)) for i in range(2)]
            PO5 = [E(nc.psum_tensor(f"PO5{i}", [128, 512], F32)) for i in range(2)]
            sems = dram["sems"]
            s_ot = sems["c_ot"]
            s_wo = sems["c_wo"]
            s_oa = sems["c_oa"]
            s_tt = sems["c_tt"]
            s_ld5 = sems["a_ld"]
            s_st5 = sems["c_st"]
            c_act = sems["a_act"]
            c_dve = sems["a_dve"]
            c_pe = sems["a_pe"]
            block = E(nc.Block())

            for dc in range(16):
                @block.sync
                def _(sy, dc=dc):
                    i = sy.dma_start(ofm[:, dc * 2048:(dc + 1) * 2048],
                                     od_d.ap()[0:S, dc * 128:(dc + 1) * 128],
                                     transpose=True)
                    s_ot.inc(i, 16)
            ot_all = s_ot.n

            wo_rec = {}
            pe_wo_done = {}
            oa_rec = {}
            evac5_rec = {}
            for g in range(8):
                @block.gpsimd
                def _(gp, g=g):
                    if g >= 2:
                        gp.wait_ge(c_pe.h, pe_wo_done[g - 2])
                    i = gp.dma_start(wsl[g % 2][:, :],
                                     wot_d.ap()[g].rearrange("p dc e -> p (dc e)"))
                    wo_rec[g] = s_wo[g % 2].inc(i, 16)

                for dco2 in range(2):
                    dch = g * 2 + dco2
                    for tt in range(4):
                        kk = dch * 4 + tt

                        @block.tensor
                        def _(t, g=g, dco2=dco2, dch=dch, tt=tt, kk=kk):
                            if dco2 == 0 and tt == 0:
                                t.wait_ge(s_wo[g % 2].h, wo_rec[g])
                                if g == 0:
                                    t.wait_ge(s_ot.h, ot_all)
                            if kk >= 2:
                                t.wait_ge(c_dve.h, evac5_rec.get(kk - 2, 0))
                            for dc in range(16):
                                i = t.matmul(
                                    PO5[kk % 2][:, :],
                                    wsl[g % 2][:, dc * 256 + dco2 * 128: dc * 256 + dco2 * 128 + 128],
                                    ofm[:, dc * 2048 + tt * 512: dc * 2048 + (tt + 1) * 512],
                                    start=(dc == 0), stop=(dc == 15))
                            c_pe.inc(i)
                            if dco2 == 1 and tt == 3:
                                pe_wo_done[g] = c_pe.n

                        pk_done = c_pe.n

                        @block.vector
                        def _(v, dch=dch, tt=tt, kk=kk, pk_done=pk_done):
                            v.wait_ge(c_pe.h, pk_done)
                            if kk >= 2:
                                v.wait_ge(s_oa[kk % 2].h, oa_rec.get(kk - 2, 0))
                            i = v.tensor_scalar(oat[kk % 2][:, tt * 512:(tt + 1) * 512],
                                                PO5[kk % 2][:, :],
                                                boc[:, dch:dch + 1], None, ALU.add)
                            evac5_rec[kk] = c_dve.inc(i)

                        @block.sync
                        def _(sy, dch=dch, tt=tt, kk=kk):
                            sy.wait_ge(c_dve.h, evac5_rec[kk])
                            i = sy.dma_start(
                                oacc_d.ap()[dch * 128:(dch + 1) * 128,
                                            tt * 512:(tt + 1) * 512],
                                oat[kk % 2][:, tt * 512:(tt + 1) * 512])
                            oa_rec[kk] = s_oa[kk % 2].inc(i, 16)

            oa_all = [s_oa[0].n, s_oa[1].n]

            ot_rec5, ld_rec5, add_rec5, st_rec5 = {}, {}, {}, {}
            for tc in range(16):
                @block.sync
                def _(sy, tc=tc):
                    if tc == 0:
                        sy.wait_ge(s_oa[0].h, oa_all[0])
                        sy.wait_ge(s_oa[1].h, oa_all[1])
                    if tc >= 2:
                        sy.wait_ge(s_st5.h, st_rec5[tc - 2])
                    i = sy.dma_start(ot5[tc % 2][:, :],
                                     oacc_d.ap()[:, tc * 128:(tc + 1) * 128],
                                     transpose=True)
                    ot_rec5[tc] = s_tt.inc(i, 16)
                    if tc >= 1:
                        sy.wait_ge(s_st5.h, st_rec5[tc - 1])
                    i = sy.dma_start(xch5[:, :],
                                     src_ap[tc * 128:(tc + 1) * 128, :])
                    ld_rec5[tc] = s_ld5.inc(i, 16)

                @block.vector
                def _(v, tc=tc):
                    v.wait_ge(s_tt.h, ot_rec5[tc])
                    v.wait_ge(s_ld5.h, ld_rec5[tc])
                    i = v.tensor_tensor(xch5[:, :], xch5[:, :],
                                        ot5[tc % 2][:, :], ALU.add)
                    add_rec5[tc] = c_dve.inc(i)

                @block.sync
                def _(sy, tc=tc):
                    sy.wait_ge(c_dve.h, add_rec5[tc])
                    i = sy.dma_start(dst_ap[tc * 128:(tc + 1) * 128, :],
                                     xch5[:, :])
                    st_rec5[tc] = s_st5.inc(i, 16)

            @block.sync
            def _(sy):
                sy.wait_ge(s_st5.h, s_st5.n)


def build(n_sub=N_SUB, dbg=False, n_exp=NE, halves=2):
    nc = bass.Bass("TRN2", target_bir_lowering=False)

    x_d = nc.dram_tensor("x", [S, D], F32, kind="ExternalInput")
    wg_d = nc.dram_tensor("wg", [NE, 16, 128, 4096], F32, kind="ExternalInput")
    wd_d = nc.dram_tensor("wd", [NE, 128, 16, 2048], F32, kind="ExternalInput")
    wr_d = nc.dram_tensor("wr", [128, 16, 8], F32, kind="ExternalInput")
    wqt_d = nc.dram_tensor("wqt", [24, 128, 16, 256], F32, kind="ExternalInput")
    wot_d = nc.dram_tensor("wot", [8, 128, 16, 256], F32, kind="ExternalInput")
    bqc_d = nc.dram_tensor("bqcd", [128, 48], F32, kind="ExternalInput")
    boc_d = nc.dram_tensor("bocd", [128, 16], F32, kind="ExternalInput")
    bqr_d = nc.dram_tensor("bqrd", [1, 6144], F32, kind="ExternalInput")
    mtri_d = nc.dram_tensor("mtri", [128, 128], F32, kind="ExternalInput")
    sel_d = nc.dram_tensor("sel", [8, 8, 128], F32, kind="ExternalInput")
    ident_d = nc.dram_tensor("ident", [128, 128], F32, kind="ExternalInput")
    y_d = nc.dram_tensor("y", [S, D], F32, kind="ExternalOutput")

    s1_d = nc.dram_tensor("s1", [S, D], F32, kind="Internal")
    s2_d = nc.dram_tensor("s2", [S, D], F32, kind="Internal")
    xln_d = nc.dram_tensor("xlnd", [S, D], BF16, kind="Internal")
    oacc_d = nc.dram_tensor("oaccd", [S, D], BF16, kind="Internal")
    qk_d = nc.dram_tensor("qkd", [4096, 2048], BF16, kind="Internal")
    od_d = nc.dram_tensor("odd", [S, D], BF16, kind="Internal")
    wgc_d = nc.dram_tensor("wgc", [NE, 16, 128, 4096], BF16, kind="Internal")
    wdc_d = nc.dram_tensor("wdc", [NE, 128, 32768], BF16, kind="Internal")

    dram = dict(xln=xln_d, oacc=oacc_d, wg=wg_d, wd=wd_d,
                qk=qk_d, od=od_d, wqt=wqt_d, wot=wot_d,
                wgc=wgc_d, wdc=wdc_d)

    with ExitStack() as ctx:
        E = ctx.enter_context
        sems = {}
        for nm in ["f_ld", "f_xw", "f_xt", "f_wd", "f_wdr", "f_cwd", "f_oa", "f_ot", "f_st",
                   "f_act", "f_dve", "f_pe", "f_dbg",
                   "a_ld", "a_xw", "a_xt", "a_bv", "a_od", "a_act", "a_dve", "a_pe",
                   "b_od", "c_ot", "c_tt", "c_st"]:
            sems[nm] = Sem(ctx, nc, nm)
        for nm in ["f_wg", "f_wu", "f_wgr", "f_wur", "f_cwg", "f_cwu", "a_qs", "a_qw",
                   "c_wo", "c_oa"]:
            sems[nm] = [Sem(ctx, nc, f"{nm}{i}") for i in range(2)]
        sems["b_qk4"] = [Sem(ctx, nc, f"bqk4{i}") for i in range(4)]
        dram["sems"] = sems
        identf32 = E(nc.sbuf_tensor("identf32", [128, 128], F32))
        wrb = E(nc.sbuf_tensor("wrb", [128, 128], BF16))
        self32 = E(nc.sbuf_tensor("self32", [8, 1024], BF16))
        neginf8 = E(nc.sbuf_tensor("neginf8", [128, 8], F32))
        epsc = E(nc.sbuf_tensor("epsc", [128, 1], F32))
        identbf = E(nc.sbuf_tensor("identbf", [128, 128], BF16))
        masktri = E(nc.sbuf_tensor("masktri", [128, 128], F32))
        bqc = E(nc.sbuf_tensor("bqc", [128, 48], F32))
        bqs = E(nc.sbuf_tensor("bqs", [128, 16], F32))
        boc = E(nc.sbuf_tensor("boc", [128, 16], F32))
        s_c = Sem(ctx, nc, "sconst")
        s_cg = Sem(ctx, nc, "sconstg")

        with nc.Block() as blk0:
            @blk0.sync
            def _(sy):
                i = sy.dma_start(identf32[:, :], ident_d.ap()[:, :])
                s_c.inc(i, 16)
                i = sy.dma_start(masktri[:, :], mtri_d.ap()[:, :])
                s_c.inc(i, 16)
                i = sy.dma_start(bqc[:, :], bqc_d.ap()[:, :])
                s_c.inc(i, 16)
                i = sy.dma_start(boc[:, :], boc_d.ap()[:, :])
                s_c.inc(i, 16)
                sy.wait_ge(s_c.h, s_c.n)

            @blk0.gpsimd
            def _(g):
                i = g.dma_start(wrb[:, :],
                                wr_d.ap().rearrange("p dc e -> p (dc e)"))
                s_cg.inc(i, 16)
                i = g.dma_start(self32[:, :].rearrange("k (e i) -> k e i", i=128),
                                sel_d.ap().rearrange("e k i -> k e i"))
                s_cg.inc(i, 16)
                i = g.dma_start(identbf[:, :], ident_d.ap()[:, :])
                s_cg.inc(i, 16)
                g.memset(neginf8[:, :], NEG)
                g.memset(epsc[:, :], 1e-5)
                g.wait_ge(s_cg.h, s_cg.n)

            @blk0.vector
            def _(v):
                v.wait_ge(s_c.h, s_c.n)
                v.tensor_scalar(bqs[:, :], bqc[:, 0:16], ISQ, None, ALU.mult)

        consts = dict(identf32=identf32, self32=self32, wrb=wrb, neginf8=neginf8,
                      epsc=epsc, identbf=identbf, masktri=masktri, bqc=bqc, bqs=bqs,
                      boc=boc, bqr_d=bqr_d)

        dbgd = None
        if dbg:
            dbgd = dict(
                xlf=nc.dram_tensor("dbg_xlf", [128, 16384], BF16, kind="ExternalOutput"),
                lg=nc.dram_tensor("dbg_lg", [128, 64], F32, kind="ExternalOutput"),
                w=nc.dram_tensor("dbg_w", [128, 64], F32, kind="ExternalOutput"),
                oac=nc.dram_tensor("dbg_oac", [128, 16384], BF16, kind="ExternalOutput"),
            )
        dst1 = y_d if n_sub == 1 else s1_d
        ffn_pair(nc, consts, dram, x_d.ap(), dst1.ap(), "f1", ('write', 'read'))

        if n_sub >= 2:
            dst2 = y_d if n_sub == 2 else s2_d
            attn_sublayer(nc, consts, dram, s1_d.ap(), dst2.ap())

        if n_sub >= 3:
            ffn_pair(nc, consts, dram, s2_d.ap(), y_d.ap(), "f3", ('read', 'read'))

    return nc


def _relayout(inputs):
    f32 = np.float32
    x = np.ascontiguousarray(np.asarray(inputs["L0_attn_x"], dtype=f32))
    Wg = np.asarray(inputs["L0_ffn_Wg"], dtype=f32)
    Wu = np.asarray(inputs["L0_ffn_Wu"], dtype=f32)
    Wd = np.asarray(inputs["L0_ffn_Wd"], dtype=f32)
    Wr = np.asarray(inputs["L0_ffn_Wr"], dtype=f32)
    shared = {}
    wg_r = Wg.reshape(8, 16, 128, 16, 128).transpose(0, 3, 2, 1, 4).reshape(8, 16, 128, 2048)
    wu_r = Wu.reshape(8, 16, 128, 16, 128).transpose(0, 3, 2, 1, 4).reshape(8, 16, 128, 2048)
    shared["wg"] = np.ascontiguousarray(np.concatenate([wg_r, wu_r], axis=3))
    shared["wd"] = np.ascontiguousarray(
        Wd.reshape(8, 16, 128, 2048).transpose(0, 2, 1, 3))
    shared["wr"] = np.ascontiguousarray(Wr.reshape(16, 128, 8).transpose(1, 0, 2))
    sel = np.zeros((8, 8, 128), dtype=f32)
    for e in range(8):
        sel[e, e, :] = 1.0
    shared["sel"] = sel
    shared["ident"] = np.ascontiguousarray(np.eye(128, dtype=f32))
    wqkv = np.asarray(inputs["L0_attn_w_qkv"], dtype=f32)
    wout = np.asarray(inputs["L0_attn_w_out"], dtype=f32)
    bqkv = np.asarray(inputs["L0_attn_b_qkv"], dtype=f32)
    bout = np.asarray(inputs["L0_attn_b_out"], dtype=f32)
    shared["wqt"] = np.ascontiguousarray(
        wqkv.reshape(24, 256, 16, 128).transpose(0, 3, 2, 1))
    shared["wot"] = np.ascontiguousarray(
        wout.reshape(8, 256, 16, 128).transpose(0, 3, 2, 1))
    shared["bqcd"] = np.ascontiguousarray(bqkv.reshape(48, 128).T)
    shared["bocd"] = np.ascontiguousarray(bout.reshape(16, 128).T)
    shared["bqrd"] = np.ascontiguousarray(bqkv.reshape(1, 6144))
    mtri = np.zeros((128, 128), dtype=f32)
    for i_ in range(128):
        mtri[i_, i_ + 1:] = NEG
    shared["mtri"] = mtri
    return x, shared


def kernel(**inputs):
    x, shared = _relayout(inputs)
    nc = build(N_SUB)
    in_maps = [dict(x=np.ascontiguousarray(x[c]), **shared) for c in range(8)]
    res = run_bass_kernel_spmd(nc, in_maps, core_ids=list(range(8)))
    return np.stack([res.results[c]["y"] for c in range(8)], axis=0)


# version 12
# speedup vs baseline: 1.0725x; 1.0725x over previous
import numpy as np
from contextlib import ExitStack

import concourse.bass as bass
import concourse.mybir as mybir
from concourse.bass_utils import run_bass_kernel_spmd

F32 = mybir.dt.float32
BF16 = mybir.dt.bfloat16
U8 = mybir.dt.uint8
AF = mybir.ActivationFunctionType
ALU = mybir.AluOpType
AX = mybir.AxisListType

S = 2048
D = 2048
NE = 8
NH = 16
HD = 128
NEG = -1.0e30
ISQ = 1.0 / float(np.sqrt(128.0))

N_SUB = 3


class Sem:
    def __init__(self, ctx, nc, name):
        self.h = ctx.enter_context(nc.semaphore(name))
        self.n = 0

    def inc(self, instr, k=1):
        instr.then_inc(self.h, k)
        self.n += k
        return self.n

    def step(self, eng, instr, k=1):
        n = self.inc(instr, k)
        eng.wait_ge(self.h, n)
        return n


def ffn_pair(nc, consts, dram, src_ap, dst_ap, tag, wmodes, next_ln_src=None):
    TH = 1024
    xln_d, oacc_d = dram["xln"], dram["oacc"]
    wg_d, wu_d, wd_d = dram["wg"], dram["wu"], dram["wd"]
    wgc_d, wuc_d, wdc_d = dram["wgc"], dram["wuc"], dram["wdc"]
    identf32, self32, wrb, neginf8 = (consts["identf32"], consts["self32"],
                                      consts["wrb"], consts["neginf8"])
    epsc = consts["epsc"]

    with ExitStack() as ctx:
        E = ctx.enter_context
        xlf = E(nc.sbuf_tensor(f"xlf{tag}", [128, 16 * TH], BF16))
        oac = E(nc.sbuf_tensor(f"oac{tag}", [128, 16 * TH], BF16))
        hta = E(nc.sbuf_tensor(f"hta{tag}", [128, 16 * TH], BF16))
        sbg = [E(nc.sbuf_tensor(f"sbg{tag}{i}", [128, 2048], BF16)) for i in range(2)]
        sbu = [E(nc.sbuf_tensor(f"sbu{tag}{i}", [128, 2048], BF16)) for i in range(2)]
        sbd = E(nc.sbuf_tensor(f"sbd{tag}", [128, 32768], BF16))
        sgt = E(nc.sbuf_tensor(f"sg{tag}", [128, TH], BF16))
        wsb = E(nc.sbuf_tensor(f"wsb{tag}", [128, TH], BF16))
        W8 = E(nc.sbuf_tensor(f"W8{tag}", [8, TH], BF16))
        wtm = E(nc.sbuf_tensor(f"wtm{tag}", [128, 64], F32))
        lgsb = E(nc.sbuf_tensor(f"lgsb{tag}", [128, 64], F32))
        xch = E(nc.sbuf_tensor(f"xch{tag}", [128, 2048], F32))
        m1t = E(nc.sbuf_tensor(f"m1t{tag}", [128, 1], F32))
        negm1 = E(nc.sbuf_tensor(f"negm1{tag}", [128, 1], F32))
        m2t = E(nc.sbuf_tensor(f"m2t{tag}", [128, 1], F32))
        d1t = E(nc.sbuf_tensor(f"d1t{tag}", [128, 1], F32))
        dent = E(nc.sbuf_tensor(f"dent{tag}", [128, 1], F32))
        rden = E(nc.sbuf_tensor(f"rden{tag}", [128, 1], F32))
        mask1 = E(nc.sbuf_tensor(f"mask1{tag}", [128, 8], U8))
        lm8 = E(nc.sbuf_tensor(f"lm8{tag}", [128, 8], F32))
        keep8 = E(nc.sbuf_tensor(f"keep8{tag}", [128, 8], F32))
        x18 = E(nc.sbuf_tensor(f"x18{tag}", [128, 8], F32))
        t8 = E(nc.sbuf_tensor(f"t8{tag}", [128, 8], F32))
        ln_tiles = {n: E(nc.sbuf_tensor(f"{n}{tag}", [128, 1], F32))
                    for n in ["ssum", "qsum", "negmu", "ex2", "musq", "var",
                              "rstd_s", "rstd"]}
        sq = E(nc.sbuf_tensor(f"sq{tag}", [128, 2048], BF16))
        xlnst = E(nc.sbuf_tensor(f"xlnst{tag}", [128, 2048], BF16))
        ot_tm = E(nc.sbuf_tensor(f"ottm{tag}", [128, 2048], BF16))
        PG = E(nc.psum_tensor(f"PG{tag}", [128, 1024], F32))
        PU = E(nc.psum_tensor(f"PU{tag}", [128, 1024], F32))
        PO = [E(nc.psum_tensor(f"PO{tag}{i}", [128, 512], F32)) for i in range(2)]
        PW = E(nc.psum_tensor(f"PW{tag}", [128, 1024], F32))
        sems = dram["sems"]
        s_ld = sems["f_ld"]
        s_xw = sems["f_xw"]
        s_xt = sems["f_xt"]
        s_wgW, s_wuW, s_wdW = sems["f_wg"], sems["f_wu"], sems["f_wd"]
        s_wgR, s_wuR, s_wdR = sems["f_wgr"], sems["f_wur"], sems["f_wdr"]
        s_cwg, s_cwu, s_cwd = sems["f_cwg"], sems["f_cwu"], sems["f_cwd"]
        s_oa = sems["f_oa"]
        s_ot = sems["f_ot"]
        s_st = sems["f_st"]
        c_act = sems["f_act"]
        c_dve = sems["f_dve"]
        c_pe = sems["f_pe"]

        block = E(nc.Block())

        ld_rec, apply_rec, red_rec, xw_rec = {}, {}, {}, {}
        xw_all = {}
        xt_all = {}
        resid_done = {}
        st_rec = {}

        def emit_ln(half, only_chunk=None, collect=None, src2=None, gate=None,
                    skip_load=False, s2_gate=None):
            t0 = (half % 2) * TH
            src_use = src_ap if src2 is None else src2
            chunks = range(8) if only_chunk is None else [only_chunk]
            for tc8 in chunks:
                tc = (half, tc8)
                stages = []

                def _s1(half=half, tc8=tc8, tc=tc, t0=t0, src_use=src_use, gate=gate):
                    @block.sync
                    def _(sy):
                        if gate is not None:
                            g0, gn = gate(tc8)
                            sy.wait_ge(g0, gn)
                        if tc8 >= 1:
                            sy.wait_ge(c_dve.h, apply_rec[(half, tc8 - 1)])
                        elif (half - 1, 7) in apply_rec:
                            sy.wait_ge(c_dve.h, apply_rec[(half - 1, 7)])
                        i = sy.dma_start(xch[:, :],
                                         src_use[t0 + tc8 * 128: t0 + (tc8 + 1) * 128, :])
                        ld_rec[tc] = s_ld.inc(i, 16)
                if not skip_load:
                    stages.append(_s1)

                def _s2(half=half, tc8=tc8, tc=tc, skip_load=skip_load, s2_gate=s2_gate):
                    @block.scalar
                    def _(sc):
                        if skip_load:
                            g0, gn = s2_gate(tc8)
                            sc.wait_ge(g0, gn)
                        else:
                            sc.wait_ge(s_ld.h, ld_rec[tc])
                        if tc8 >= 1:
                            sc.wait_ge(c_dve.h, red_rec[(half, tc8 - 1)])
                        i = sc.activation(sq[:, :], xch[:, :], AF.Square)
                        c_act.inc(i)
                        sq_done_box[0] = c_act.n
                stages.append(_s2)
                sq_done_box = [None]

                def _s3(half=half, tc8=tc8, tc=tc):
                    sq_done = sq_done_box[0]

                    @block.vector
                    def _(v):
                        v.wait_ge(c_act.h, sq_done)
                        if tc8 >= 1:
                            v.wait_ge(c_dve.h, apply_rec[(half, tc8 - 1)])
                        i = v.reduce_sum(ln_tiles["ssum"][:, :], xch[:, :], AX.X)
                        c_dve.step(v, i)
                        i = v.reduce_sum(ln_tiles["qsum"][:, :], sq[:, :], AX.X)
                        red_rec[tc] = c_dve.step(v, i)
                        i = v.tensor_scalar(ln_tiles["negmu"][:, :], ln_tiles["ssum"][:, :],
                                            -1.0 / D, None, ALU.mult)
                        c_dve.step(v, i)
                        i = v.tensor_scalar(ln_tiles["ex2"][:, :], ln_tiles["qsum"][:, :],
                                            1.0 / D, None, ALU.mult)
                        c_dve.step(v, i)
                        i = v.tensor_tensor(ln_tiles["musq"][:, :], ln_tiles["negmu"][:, :],
                                            ln_tiles["negmu"][:, :], ALU.mult)
                        c_dve.step(v, i)
                        i = v.tensor_tensor(ln_tiles["var"][:, :], ln_tiles["ex2"][:, :],
                                            ln_tiles["musq"][:, :], ALU.subtract)
                        c_dve.inc(i)
                        var_done_box[0] = c_dve.n
                stages.append(_s3)
                var_done_box = [None]

                def _s4(tc=tc):
                    var_done = var_done_box[0]

                    @block.scalar
                    def _(sc):
                        sc.wait_ge(c_dve.h, var_done)
                        i = sc.activation(ln_tiles["rstd_s"][:, :], ln_tiles["var"][:, :],
                                          AF.Sqrt, bias=epsc[:, :])
                        c_act.inc(i)
                        sqrt_done_box[0] = c_act.n
                stages.append(_s4)
                sqrt_done_box = [None]

                def _s5(half=half, tc8=tc8, tc=tc):
                    sqrt_done = sqrt_done_box[0]

                    @block.vector
                    def _(v):
                        v.wait_ge(c_act.h, sqrt_done)
                        i = v.reciprocal(ln_tiles["rstd"][:, :], ln_tiles["rstd_s"][:, :])
                        c_dve.step(v, i)
                        prev = (half, tc8 - 1) if tc8 >= 1 else (half - 1, 7)
                        if prev in xw_rec:
                            v.wait_ge(s_xw.h, xw_rec[prev])
                        i = v.tensor_scalar(xlnst[:, :], xch[:, :],
                                            ln_tiles["negmu"][:, :], ln_tiles["rstd"][:, :],
                                            ALU.add, ALU.mult)
                        apply_rec[tc] = c_dve.inc(i)
                stages.append(_s5)

                def _s6(half=half, tc8=tc8, tc=tc, t0=t0):
                    @block.sync
                    def _(sy):
                        sy.wait_ge(c_dve.h, apply_rec[tc])
                        i = sy.dma_start(xln_d.ap()[t0 + tc8 * 128: t0 + (tc8 + 1) * 128, :],
                                         xlnst[:, :])
                        xw_rec[tc] = s_xw.inc(i, 16)
                    if tc8 == 7:
                        xw_all[half] = s_xw.n
                stages.append(_s6)

                if collect is None:
                    for st in stages:
                        st()
                else:
                    collect.extend(stages)

        def emit_xt(half, wait_pe=None):
            t0 = half * TH
            for dc in range(16):
                @block.sync
                def _(sy, half=half, dc=dc, t0=t0, wait_pe=wait_pe):
                    if dc == 0:
                        sy.wait_ge(s_xw.h, xw_all[half])
                        if wait_pe is not None:
                            sy.wait_ge(c_pe.h, wait_pe)
                    i = sy.dma_start(xlf[:, dc * TH:(dc + 1) * TH],
                                     xln_d.ap()[t0:t0 + TH, dc * 128:(dc + 1) * 128],
                                     transpose=True)
                    s_xt.inc(i, 16)
            xt_all[half] = s_xt.n

        lg_done, lg_evac, w_done = {}, {}, {}
        tr_done, tr_evac = {}, {}
        w8_all = {}

        def emit_router(half):
            for ts_ in range(8):
                reg = PW[:, 0:8] if ts_ % 2 == 0 else PW[:, 512:520]
                key = (half, ts_)

                @block.tensor
                def _(t, half=half, ts_=ts_, reg=reg, key=key):
                    if ts_ == 0:
                        t.wait_ge(s_xt.h, xt_all[half])
                        if half == 1:
                            t.wait_ge(c_dve.h, wbc_evac_rec[7])
                    if ts_ >= 2:
                        t.wait_ge(c_dve.h, lg_evac[(half, ts_ - 2)])
                    for dc in range(16):
                        i = t.matmul(reg, xlf[:, dc * TH + ts_ * 128: dc * TH + (ts_ + 1) * 128],
                                     wrb[:, dc * 8:(dc + 1) * 8],
                                     start=(dc == 0), stop=(dc == 15))
                    lg_done[key] = c_pe.inc(i)

                @block.vector
                def _(v, key=key, reg=reg, half=half, ts_=ts_):
                    v.wait_ge(c_pe.h, lg_done[key])
                    if half == 1 and ts_ == 0:
                        v.wait_ge(c_dve.h, ht_rec[8 * 16 - 1])
                    i = v.tensor_copy(lgsb[:, ts_ * 8:(ts_ + 1) * 8], reg)
                    lg_evac[key] = c_dve.inc(i)

            for ts_ in range(8):
                key = (half, ts_)

                @block.vector
                def _(v, half=half, ts_=ts_, key=key):
                    Ls = lgsb[:, ts_ * 8:(ts_ + 1) * 8]
                    v.wait_ge(c_dve.h, lg_evac[key])
                    prev = (half, ts_ - 1) if ts_ >= 1 else (half - 1, 7)
                    if prev in w_done:
                        v.wait_ge(c_dve.h, w_done[prev])
                    i = v.reduce_max(m1t[:, :], Ls, AX.X)
                    c_dve.step(v, i)
                    i = v.tensor_scalar(negm1[:, :], m1t[:, :], -1.0, None, ALU.mult)
                    c_dve.step(v, i)
                    i = v.tensor_scalar(mask1[:, :], Ls, m1t[:, :], None, ALU.is_ge)
                    c_dve.step(v, i)
                    i = v.select(lm8[:, :], mask1[:, :], neginf8[:, :], Ls, add_drain=True)
                    c_dve.step(v, i)
                    i = v.reduce_max(m2t[:, :], lm8[:, :], AX.X)
                    c_dve.inc(i)

                mx_done = c_dve.n

                @block.scalar
                def _(sc, ts_=ts_, mx_done=mx_done):
                    sc.wait_ge(c_dve.h, mx_done)
                    sc.activation(x18[:, :], lgsb[:, ts_ * 8:(ts_ + 1) * 8], AF.Exp,
                                  bias=negm1[:, :])
                    i = sc.activation(d1t[:, :], m2t[:, :], AF.Exp, bias=negm1[:, :])
                    c_act.inc(i)

                exp_done = c_act.n

                @block.vector
                def _(v, ts_=ts_, key=key, exp_done=exp_done):
                    v.wait_ge(c_act.h, exp_done)
                    i = v.tensor_scalar(dent[:, :], d1t[:, :], 1.0, None, ALU.add)
                    c_dve.step(v, i)
                    i = v.reciprocal(rden[:, :], dent[:, :])
                    c_dve.step(v, i)
                    i = v.tensor_scalar(keep8[:, :], lgsb[:, ts_ * 8:(ts_ + 1) * 8],
                                        m2t[:, :], None, ALU.is_ge)
                    c_dve.step(v, i)
                    i = v.tensor_tensor(t8[:, :], x18[:, :], keep8[:, :], ALU.mult)
                    c_dve.step(v, i)
                    i = v.tensor_scalar(wtm[:, ts_ * 8:(ts_ + 1) * 8], t8[:, :],
                                        rden[:, :], None, ALU.mult)
                    w_done[key] = c_dve.inc(i)

            for ts_ in range(8):
                reg = PW[0:8, 0:128] if ts_ % 2 == 0 else PW[0:8, 512:640]
                key = (half, ts_)

                @block.tensor
                def _(t, ts_=ts_, reg=reg, key=key, half=half):
                    t.wait_ge(c_dve.h, w_done[key])
                    if ts_ >= 2:
                        t.wait_ge(c_dve.h, tr_evac[(half, ts_ - 2)])
                    i = t.transpose(reg, wtm[:, ts_ * 8:(ts_ + 1) * 8], identf32[:, :])
                    tr_done[key] = c_pe.inc(i)

                @block.vector
                def _(v, ts_=ts_, reg=reg, key=key):
                    v.wait_ge(c_pe.h, tr_done[key])
                    i = v.tensor_copy(W8[0:8, ts_ * 128:(ts_ + 1) * 128], reg)
                    tr_evac[key] = c_dve.inc(i)
            w8_all[half] = c_dve.n

        g_rec, u_rec = {}, {}
        wg_rec, wu_rec, wd_rec = {}, {}, {}
        cwg_rec, cwu_rec, cwd_rec = {}, {}, {}
        wd_used_rec = {}
        emitted_loads = set()

        def emit_wload(eh, fc):
            if (eh, fc) in emitted_loads:
                return
            emitted_loads.add((eh, fc))
            half = eh // 8
            e = eh % 8
            k = eh * 16 + fc
            wmode = wmodes[half]
            if wmode == 'write':
                @block.gpsimd
                def _(g, e=e, fc=fc, k=k):
                    if k >= 2:
                        g.wait_ge(c_pe.h, g_rec[k - 2])
                        g.wait_ge(s_cwg[fc % 2].h, cwg_rec[k - 2])
                    i = g.dma_start(sbg[fc % 2][:, :],
                                    wg_d.ap()[e, fc].rearrange("p dc f -> p (dc f)"))
                    wg_rec[k] = s_wgW[fc % 2].inc(i, 16)
                    if k >= 2:
                        g.wait_ge(c_pe.h, u_rec[k - 2])
                        g.wait_ge(s_cwu[fc % 2].h, cwu_rec[k - 2])
                    i = g.dma_start(sbu[fc % 2][:, :],
                                    wu_d.ap()[e, fc].rearrange("p dc f -> p (dc f)"))
                    wu_rec[k] = s_wuW[fc % 2].inc(i, 16)

                @block.sync
                def _(sy, e=e, fc=fc, k=k):
                    sy.wait_ge(s_wgW[fc % 2].h, wg_rec[k])
                    i = sy.dma_start(wgc_d.ap()[e, fc], sbg[fc % 2][:, :])
                    cwg_rec[k] = s_cwg[fc % 2].inc(i, 16)
                    sy.wait_ge(s_wuW[fc % 2].h, wu_rec[k])
                    i = sy.dma_start(wuc_d.ap()[e, fc], sbu[fc % 2][:, :])
                    cwu_rec[k] = s_cwu[fc % 2].inc(i, 16)
            else:
                @block.sync
                def _(sy, e=e, fc=fc, k=k):
                    if k >= 2:
                        sy.wait_ge(c_pe.h, g_rec[k - 2])
                        if wmodes[(k - 2) // 128] == 'write':
                            sy.wait_ge(s_cwg[fc % 2].h, cwg_rec[k - 2])
                    i = sy.dma_start(sbg[fc % 2][:, :], wgc_d.ap()[e, fc])
                    wg_rec[k] = s_wgR[fc % 2].inc(i, 16)
                    if k >= 2:
                        sy.wait_ge(c_pe.h, u_rec[k - 2])
                        if wmodes[(k - 2) // 128] == 'write':
                            sy.wait_ge(s_cwu[fc % 2].h, cwu_rec[k - 2])
                    i = sy.dma_start(sbu[fc % 2][:, :], wuc_d.ap()[e, fc])
                    wu_rec[k] = s_wuR[fc % 2].inc(i, 16)

        def emit_wd_load(eh):
            if ("wd", eh) in emitted_loads:
                return
            emitted_loads.add(("wd", eh))
            half = eh // 8
            e = eh % 8
            wmode = wmodes[half]
            if wmode == 'write':
                @block.gpsimd
                def _(g, e=e, eh=eh):
                    if eh >= 1:
                        g.wait_ge(c_pe.h, wd_used_rec[eh - 1])
                        g.wait_ge(s_cwd.h, cwd_rec[eh - 1])
                    for q in range(4):
                        i = g.dma_start(sbd[:, q * 8192:(q + 1) * 8192],
                                        wd_d.ap()[e][:, q * 4:(q + 1) * 4, :]
                                        .rearrange("p fk do -> p (fk do)"))
                        s_wdW.inc(i, 16)
                    wd_rec[eh] = s_wdW.n

                @block.sync
                def _(sy, e=e, eh=eh):
                    sy.wait_ge(s_wdW.h, wd_rec[eh])
                    for q in range(4):
                        i = sy.dma_start(wdc_d.ap()[e, :, q * 8192:(q + 1) * 8192],
                                         sbd[:, q * 8192:(q + 1) * 8192])
                        s_cwd.inc(i, 16)
                    cwd_rec[eh] = s_cwd.n
            else:
                @block.sync
                def _(sy, e=e, eh=eh):
                    if eh >= 1:
                        sy.wait_ge(c_pe.h, wd_used_rec[eh - 1])
                        if wmodes[(eh - 1) // 8] == 'write':
                            sy.wait_ge(s_cwd.h, cwd_rec[eh - 1])
                    for q in range(4):
                        i = sy.dma_start(sbd[:, q * 8192:(q + 1) * 8192],
                                         wdc_d.ap()[e, :, q * 8192:(q + 1) * 8192])
                        s_wdR.inc(i, 16)
                    wd_rec[eh] = s_wdR.n

        def wd_sem(eh):
            return s_wdW if wmodes[eh // 8] == 'write' else s_wdR

        def wg_sem(eh, fc):
            return (s_wgW if wmodes[eh // 8] == 'write' else s_wgR)[fc % 2]

        def wu_sem(eh, fc):
            return (s_wuW if wmodes[eh // 8] == 'write' else s_wuR)[fc % 2]

        sil_rec, h1_rec, ht_rec = {}, {}, {}
        wbc_rec, wbc_evac_rec = {}, {}
        po_evac_rec = {}
        oa_ready = {}

        def emit_expert(eh):
            half = eh // 8
            e = eh % 8
            emit_wload(eh, 0)
            emit_wload(eh, 1)

            @block.tensor
            def _(t, eh=eh, half=half):
                if eh == 0:
                    t.wait_ge(c_dve.h, w8_all[0])
                elif eh == 8:
                    t.wait_ge(c_dve.h, w8_all[1])
                    t.wait_ge(c_dve.h, wbc_evac_rec[eh - 1])
                else:
                    t.wait_ge(c_dve.h, wbc_evac_rec[eh - 1])
                for th in range(2):
                    i = t.matmul(PW[:, th * 512:(th + 1) * 512],
                                 self32[0:8, (eh % 8) * 128:((eh % 8) + 1) * 128],
                                 W8[0:8, th * 512:(th + 1) * 512],
                                 start=True, stop=True)
                wbc_rec[eh] = c_pe.inc(i)

            @block.vector
            def _(v, eh=eh):
                v.wait_ge(c_pe.h, wbc_rec[eh])
                if eh >= 1:
                    v.wait_ge(c_dve.h, ht_rec[eh * 16 - 1])
                i = v.tensor_scalar(wsb[:, :], PW[:, 0:1024], 0.5, None, ALU.mult)
                wbc_evac_rec[eh] = c_dve.inc(i)

            for fc in range(16):
                emit_wload(eh, fc)
                if fc == 4:
                    emit_wd_load(eh)
                if ln_stage_q:
                    ln_stage_q.pop(0)()
                k = eh * 16 + fc

                @block.tensor
                def _(t, eh=eh, half=half, fc=fc, k=k):
                    t.wait_ge(wg_sem(eh, fc).h, wg_rec[k])
                    if k >= 1:
                        t.wait_ge(c_act.h, sil_rec[k - 1])
                    for dc in range(16):
                        for tt in range(2):
                            i = t.matmul(PG[:, tt * 512:(tt + 1) * 512],
                                         sbg[fc % 2][:, dc * 128:(dc + 1) * 128],
                                         xlf[:, dc * TH + tt * 512: dc * TH + (tt + 1) * 512],
                                         start=(dc == 0), stop=(dc == 15))
                    g_rec[k] = c_pe.inc(i)
                    t.wait_ge(wu_sem(eh, fc).h, wu_rec[k])
                    if k >= 1:
                        t.wait_ge(c_dve.h, h1_rec[k - 1])
                    for dc in range(16):
                        for tt in range(2):
                            i = t.matmul(PU[:, tt * 512:(tt + 1) * 512],
                                         sbu[fc % 2][:, dc * 128:(dc + 1) * 128],
                                         xlf[:, dc * TH + tt * 512: dc * TH + (tt + 1) * 512],
                                         start=(dc == 0), stop=(dc == 15))
                    u_rec[k] = c_pe.inc(i)

                @block.scalar
                def _(sc, k=k):
                    sc.wait_ge(c_pe.h, g_rec[k])
                    if k >= 1:
                        sc.wait_ge(c_dve.h, h1_rec[k - 1])
                    i = sc.activation(sgt[:, :], PG[:, 0:1024], AF.Silu)
                    sil_rec[k] = c_act.inc(i)

                @block.vector
                def _(v, k=k, fc=fc):
                    v.wait_ge(c_act.h, sil_rec[k])
                    v.wait_ge(c_pe.h, u_rec[k])
                    if k >= 1:
                        v.wait_ge(c_dve.h, ht_rec[k - 1])
                    hsl = hta[:, fc * TH:(fc + 1) * TH]
                    i = v.tensor_tensor(hsl, sgt[:, :], PU[:, 0:1024], ALU.mult)
                    h1_rec[k] = c_dve.step(v, i)
                    i = v.tensor_tensor(hsl, hsl, wsb[:, :], ALU.mult)
                    ht_rec[k] = c_dve.inc(i)

            last_ht = ht_rec[eh * 16 + 15]
            for dco in range(16):
                for tt in range(2):
                    kk = eh * 32 + dco * 2 + tt

                    @block.tensor
                    def _(t, eh=eh, dco=dco, tt=tt, kk=kk, last_ht=last_ht):
                        if dco == 0 and tt == 0:
                            t.wait_ge(wd_sem(eh).h, wd_rec[eh])
                            t.wait_ge(c_dve.h, last_ht)
                        if kk >= 2:
                            t.wait_ge(c_dve.h, po_evac_rec[kk - 2])
                        for fk in range(16):
                            i = t.matmul(PO[kk % 2][:, :],
                                         sbd[:, fk * 2048 + dco * 128: fk * 2048 + (dco + 1) * 128],
                                         hta[:, fk * TH + tt * 512: fk * TH + (tt + 1) * 512],
                                         start=(fk == 0), stop=(fk == 15))
                        c_pe.inc(i)

                    po_done = c_pe.n

                    @block.vector
                    def _(v, eh=eh, dco=dco, tt=tt, kk=kk, po_done=po_done):
                        v.wait_ge(c_pe.h, po_done)
                        e8 = eh % 8
                        if e8 >= 1:
                            v.wait_ge(c_dve.h, po_evac_rec[kk - 32])
                        elif eh == 8:
                            v.wait_ge(s_oa.h, oa_ready[0])
                        dsl = oac[:, dco * TH + tt * 512: dco * TH + (tt + 1) * 512]
                        if e8 == 0:
                            i = v.tensor_copy(dsl, PO[kk % 2][:, :])
                        else:
                            i = v.tensor_tensor(dsl, dsl, PO[kk % 2][:, :], ALU.add)
                        po_evac_rec[kk] = c_dve.inc(i)

            wd_used_rec[eh] = c_pe.n

        def emit_oa_out(half):
            oac_done = c_dve.n
            t0 = half * TH
            for g4 in range(4):
                @block.sync
                def _(sy, g4=g4, t0=t0, oac_done=oac_done):
                    if g4 == 0:
                        sy.wait_ge(c_dve.h, oac_done)
                    i = sy.dma_start(
                        oacc_d.ap()[:, t0:t0 + TH]
                        .rearrange("(dc p) t -> p dc t", p=128)[:, g4 * 4:(g4 + 1) * 4, :],
                        oac[:, :].rearrange("p (dc t) -> p dc t", t=TH)[:, g4 * 4:(g4 + 1) * 4, :])
                    s_oa.inc(i, 16)
            oa_ready[half] = s_oa.n

        def emit_resid(half, queue="sync", fuse_ln_src=None):
            t0 = half * TH
            ot_rec, res_ld_rec, xnew_rec = {}, {}, {}
            dec = block.sync
            for tc8 in range(8):
                @dec
                def _(sy, half=half, tc8=tc8, t0=t0):
                    if tc8 == 0:
                        sy.wait_ge(s_oa.h, oa_ready[half])
                        if half == 0:
                            sy.wait_ge(c_dve.h, apply_rec[(1, 7)])
                        else:
                            sy.wait_ge(s_st.h, st_rec[(0, 7)])
                    if tc8 >= 1:
                        sy.wait_ge(c_dve.h, xnew_rec[tc8 - 1])
                    i = sy.dma_start(ot_tm[:, :],
                                     oacc_d.ap()[:, t0 + tc8 * 128: t0 + (tc8 + 1) * 128],
                                     transpose=True)
                    ot_rec[tc8] = s_ot.inc(i, 16)
                    if tc8 >= 1:
                        sy.wait_ge(s_st.h, st_rec[(half, tc8 - 1)])
                        if fuse_ln_src is not None:
                            sy.wait_ge(c_dve.h, apply_rec[(3, tc8 - 1)])
                    i = sy.dma_start(xch[:, :],
                                     src_ap[t0 + tc8 * 128: t0 + (tc8 + 1) * 128, :])
                    res_ld_rec[tc8] = s_ld.inc(i, 16)

                @block.vector
                def _(v, half=half, tc8=tc8):
                    v.wait_ge(s_ot.h, ot_rec[tc8])
                    v.wait_ge(s_ld.h, res_ld_rec[tc8])
                    i = v.tensor_tensor(xch[:, :], xch[:, :],
                                        ot_tm[:, :], ALU.add)
                    xnew_rec[tc8] = c_dve.inc(i)

                @dec
                def _(sy, half=half, tc8=tc8, t0=t0):
                    sy.wait_ge(c_dve.h, xnew_rec[tc8])
                    i = sy.dma_start(dst_ap[t0 + tc8 * 128: t0 + (tc8 + 1) * 128, :],
                                     xch[:, :])
                    st_rec[(half, tc8)] = s_st.inc(i, 16)

                if fuse_ln_src is not None:
                    xr = xnew_rec[tc8]
                    emit_ln(3, only_chunk=tc8, src2=fuse_ln_src, skip_load=True,
                            s2_gate=lambda _tc, xr=xr: (c_dve.h, xr))
            resid_done[half] = c_dve.n

        ln_stage_q = []
        ln2_stage_q = []
        emit_ln(0)
        emit_xt(0)
        emit_router(0)
        emit_ln(1, collect=ln_stage_q)
        for eh in range(8):
            emit_expert(eh)
        while ln_stage_q:
            ln_stage_q.pop(0)()
        emit_xt(1, wait_pe=u_rec[8 * 16 - 1])
        emit_wload(8, 0)
        emit_wload(8, 1)
        emit_oa_out(0)
        emit_router(1)
        emit_wd_load(8)
        emit_resid(0)
        if next_ln_src is not None:
            emit_ln(2, collect=ln2_stage_q, src2=next_ln_src,
                    gate=lambda tc8: (s_st.h, st_rec[(0, 7)]))
        ln_stage_q = ln2_stage_q
        for eh in range(8, 16):
            emit_expert(eh)
        while ln_stage_q:
            ln_stage_q.pop(0)()
        emit_oa_out(1)
        emit_resid(1, fuse_ln_src=next_ln_src)

        @block.sync
        def _(sy):
            sy.wait_ge(s_st.h, s_st.n)
            if next_ln_src is not None:
                sy.wait_ge(s_xw.h, s_xw.n)
            if 'write' in wmodes:
                sy.wait_ge(s_cwg[0].h, s_cwg[0].n)
                sy.wait_ge(s_cwg[1].h, s_cwg[1].n)
                sy.wait_ge(s_cwu[0].h, s_cwu[0].n)
                sy.wait_ge(s_cwu[1].h, s_cwu[1].n)
                sy.wait_ge(s_cwd.h, s_cwd.n)


def attn_sublayer(nc, consts, dram, src_ap, dst_ap, skip_ln=False):
    xln_d, oacc_d = dram["xln"], dram["oacc"]
    qk_d, od_d = dram["qk"], dram["od"]
    wqt_d, wot_d = dram["wqt"], dram["wot"]
    identbf, masktri, epsc = consts["identbf"], consts["masktri"], consts["epsc"]
    bqc, bqs, boc = consts["bqc"], consts["bqs"], consts["boc"]
    bqr_d = consts["bqr_d"]

    with ExitStack() as ctxS:
        ES = ctxS.enter_context
        vtm = ES(nc.sbuf_tensor("vtm", [128, 32768], BF16))
        with ExitStack() as ctx:
            E = ctx.enter_context
            xlf = E(nc.sbuf_tensor("xlfA", [128, 32768], BF16))
            qsl = [E(nc.sbuf_tensor(f"qslA{i}", [128, 4096], BF16)) for i in range(2)]
            qkst = [E(nc.sbuf_tensor(f"qkstA{i}", [128, 2048], BF16)) for i in range(2)]
            xch = E(nc.sbuf_tensor("xchA", [128, 2048], F32))
            ln_tiles = {n: E(nc.sbuf_tensor(f"{n}A", [128, 1], F32))
                        for n in ["ssum", "qsum", "negmu", "ex2", "musq", "var",
                                  "rstd_s", "rstd"]}
            sq = E(nc.sbuf_tensor("sqA", [128, 2048], BF16))
            xlnst = E(nc.sbuf_tensor("xlnstA", [128, 2048], BF16))
            bqrv = E(nc.sbuf_tensor("bqrv", [1, 2048], BF16))
            ones1 = E(nc.sbuf_tensor("ones1", [1, 128], BF16))
            PQ = E(nc.psum_tensor("PQA", [128, 2048], F32))
            PV = [E(nc.psum_tensor(f"PVA{i}", [128, 512], F32)) for i in range(2)]
            sems = dram["sems"]
            s_ld = sems["a_ld"]
            s_xw = sems["f_xw"] if skip_ln else sems["a_xw"]
            s_xt = sems["a_xt"]
            s_qs = sems["a_qs"]
            s_qw = sems["a_qw"]
            c_act = sems["a_act"]
            c_dve = sems["a_dve"]
            c_pe = sems["a_pe"]
            s_bv = sems["a_bv"]
            block = E(nc.Block())

            @block.gpsimd
            def _(g):
                i = g.dma_start(bqrv[:, :], bqr_d.ap()[0:1, 4096:6144])
                s_bv.inc(i, 16)
                g.memset(ones1[:, :], 1.0)
                g.wait_ge(s_bv.h, s_bv.n)

            if skip_ln:
                xw_all = dram["ln2_xw_n"]
            else:
                ld_rec, apply_rec, red_rec, xw_rec = {}, {}, {}, {}
                for tc in range(16):
                    @block.sync
                    def _(sy, tc=tc):
                        if tc >= 1:
                            sy.wait_ge(c_dve.h, apply_rec[tc - 1])
                        i = sy.dma_start(xch[:, :],
                                         src_ap[tc * 128:(tc + 1) * 128, :])
                        ld_rec[tc] = s_ld.inc(i, 16)
    
                    @block.scalar
                    def _(sc, tc=tc):
                        sc.wait_ge(s_ld.h, ld_rec[tc])
                        if tc >= 1:
                            sc.wait_ge(c_dve.h, red_rec[tc - 1])
                        i = sc.activation(sq[:, :], xch[:, :], AF.Square)
                        c_act.inc(i)
    
                    sq_done = c_act.n
    
                    @block.vector
                    def _(v, tc=tc, sq_done=sq_done):
                        v.wait_ge(c_act.h, sq_done)
                        if tc >= 1:
                            v.wait_ge(c_dve.h, apply_rec[tc - 1])
                        i = v.reduce_sum(ln_tiles["ssum"][:, :], xch[:, :], AX.X)
                        c_dve.step(v, i)
                        i = v.reduce_sum(ln_tiles["qsum"][:, :], sq[:, :], AX.X)
                        red_rec[tc] = c_dve.step(v, i)
                        i = v.tensor_scalar(ln_tiles["negmu"][:, :], ln_tiles["ssum"][:, :],
                                            -1.0 / D, None, ALU.mult)
                        c_dve.step(v, i)
                        i = v.tensor_scalar(ln_tiles["ex2"][:, :], ln_tiles["qsum"][:, :],
                                            1.0 / D, None, ALU.mult)
                        c_dve.step(v, i)
                        i = v.tensor_tensor(ln_tiles["musq"][:, :], ln_tiles["negmu"][:, :],
                                            ln_tiles["negmu"][:, :], ALU.mult)
                        c_dve.step(v, i)
                        i = v.tensor_tensor(ln_tiles["var"][:, :], ln_tiles["ex2"][:, :],
                                            ln_tiles["musq"][:, :], ALU.subtract)
                        c_dve.inc(i)
    
                    var_done = c_dve.n
    
                    @block.scalar
                    def _(sc, tc=tc, var_done=var_done):
                        sc.wait_ge(c_dve.h, var_done)
                        i = sc.activation(ln_tiles["rstd_s"][:, :], ln_tiles["var"][:, :],
                                          AF.Sqrt, bias=epsc[:, :])
                        c_act.inc(i)
    
                    sqrt_done = c_act.n
    
                    @block.vector
                    def _(v, tc=tc, sqrt_done=sqrt_done):
                        v.wait_ge(c_act.h, sqrt_done)
                        i = v.reciprocal(ln_tiles["rstd"][:, :], ln_tiles["rstd_s"][:, :])
                        c_dve.step(v, i)
                        if tc >= 1:
                            v.wait_ge(s_xw.h, xw_rec[tc - 1])
                        i = v.tensor_scalar(xlnst[:, :], xch[:, :],
                                            ln_tiles["negmu"][:, :], ln_tiles["rstd"][:, :],
                                            ALU.add, ALU.mult)
                        apply_rec[tc] = c_dve.inc(i)
    
                    @block.sync
                    def _(sy, tc=tc):
                        sy.wait_ge(c_dve.h, apply_rec[tc])
                        i = sy.dma_start(xln_d.ap()[tc * 128:(tc + 1) * 128, :],
                                         xlnst[:, :])
                        xw_rec[tc] = s_xw.inc(i, 16)
    
                xw_all = s_xw.n
            for dc in range(16):
                @block.sync
                def _(sy, dc=dc):
                    if dc == 0:
                        sy.wait_ge(s_xw.h, xw_all)
                    i = sy.dma_start(xlf[:, dc * 2048:(dc + 1) * 2048],
                                     xln_d.ap()[0:S, dc * 128:(dc + 1) * 128],
                                     transpose=True)
                    s_xt.inc(i, 16)
            xt_all = s_xt.n

            qs_rec = {}
            pe_slab_done = {}

            def load_slab(idx, eb):
                @block.gpsimd
                def _(g, idx=idx, eb=eb):
                    if idx >= 2:
                        g.wait_ge(c_pe.h, pe_slab_done[idx - 2])
                    i = g.dma_start(qsl[idx % 2][:, :],
                                    wqt_d.ap()[eb].rearrange("p dc e -> p (dc e)"))
                    qs_rec[idx] = s_qs[idx % 2].inc(i, 16)

            vevac_rec = {}
            for ebv in range(8):
                load_slab(ebv, 16 + ebv)
                for tc in range(16):
                    kk = ebv * 16 + tc

                    @block.tensor
                    def _(t, ebv=ebv, tc=tc, kk=kk):
                        if tc == 0:
                            t.wait_ge(s_qs[ebv % 2].h, qs_rec[ebv])
                            if ebv == 0:
                                t.wait_ge(s_xt.h, xt_all)
                        if kk >= 2:
                            t.wait_ge(c_dve.h, vevac_rec[kk - 2])
                        for dc in range(16):
                            t.matmul(PV[kk % 2][:, 0:256],
                                     xlf[:, dc * 2048 + tc * 128: dc * 2048 + (tc + 1) * 128],
                                     qsl[ebv % 2][:, dc * 256:(dc + 1) * 256],
                                     start=(dc == 0), stop=False)
                        i = t.matmul(PV[kk % 2][:, 0:256], ones1[0:1, :],
                                     bqrv[0:1, ebv * 256:(ebv + 1) * 256],
                                     start=False, stop=True)
                        c_pe.inc(i)
                        if tc == 15:
                            pe_slab_done[ebv] = c_pe.n

                    pv_done = c_pe.n

                    @block.vector
                    def _(v, ebv=ebv, tc=tc, kk=kk, pv_done=pv_done):
                        v.wait_ge(c_pe.h, pv_done)
                        i = v.tensor_copy(
                            vtm[:, tc * 2048 + ebv * 256: tc * 2048 + (ebv + 1) * 256],
                            PV[kk % 2][:, 0:256])
                        vevac_rec[kk] = c_dve.inc(i)

            qk_evac_rec = {}
            qw_rec = {}
            for ebq in range(16):
                sidx = 8 + ebq
                load_slab(sidx, ebq)
                for ec2 in range(2):
                    ec = ebq * 2 + ec2
                    gi = ebq * 2 + ec2

                    @block.tensor
                    def _(t, ebq=ebq, ec2=ec2, sidx=sidx, gi=gi):
                        if ec2 == 0:
                            t.wait_ge(s_qs[sidx % 2].h, qs_rec[sidx])
                        if gi >= 1:
                            t.wait_ge(c_dve.h, qk_evac_rec[gi - 1])
                        for tt in range(4):
                            for dc in range(16):
                                i = t.matmul(
                                    PQ[:, tt * 512:(tt + 1) * 512],
                                    qsl[sidx % 2][:, dc * 256 + ec2 * 128: dc * 256 + ec2 * 128 + 128],
                                    xlf[:, dc * 2048 + tt * 512: dc * 2048 + (tt + 1) * 512],
                                    start=(dc == 0), stop=(dc == 15))
                        c_pe.inc(i)
                        if ec2 == 1:
                            pe_slab_done[sidx] = c_pe.n

                    pq_done = c_pe.n

                    @block.vector
                    def _(v, ec=ec, gi=gi, pq_done=pq_done):
                        v.wait_ge(c_pe.h, pq_done)
                        if gi >= 2:
                            v.wait_ge(s_qw[gi % 2].h, qw_rec[gi - 2])
                        if ec < 16:
                            bias, scale = bqs[:, ec:ec + 1], ISQ
                        else:
                            bias, scale = bqc[:, ec:ec + 1], 1.0
                        for tt in range(4):
                            i = v.tensor_scalar(qkst[gi % 2][:, tt * 512:(tt + 1) * 512],
                                                PQ[:, tt * 512:(tt + 1) * 512],
                                                bias, scale, ALU.add, ALU.mult)
                        qk_evac_rec[gi] = c_dve.inc(i)

                    @block.sync
                    def _(sy, ec=ec, gi=gi):
                        sy.wait_ge(c_dve.h, qk_evac_rec[gi])
                        i = sy.dma_start(qk_d.ap()[ec * 128:(ec + 1) * 128, :],
                                         qkst[gi % 2][:, :])
                        qw_rec[gi] = s_qw[gi % 2].inc(i, 16)

            @block.sync
            def _(sy):
                sy.wait_ge(s_qw[0].h, s_qw[0].n)
                sy.wait_ge(s_qw[1].h, s_qw[1].n)

        with ExitStack() as ctx:
            E = ctx.enter_context
            osb = E(nc.sbuf_tensor("osb", [128, 32768], BF16))
            qkh = [E(nc.sbuf_tensor(f"qkh{i}", [128, 4096], BF16)) for i in range(4)]
            pbf = [E(nc.sbuf_tensor(f"pbf{i}", [128, 2048], BF16)) for i in range(4)]
            ptsb = [E(nc.sbuf_tensor(f"ptsb{i}", [128, 2048], BF16)) for i in range(4)]
            mrow = [E(nc.sbuf_tensor(f"mrow{i}", [128, 1], F32)) for i in range(4)]
            negm = [E(nc.sbuf_tensor(f"negm{i}", [128, 1], F32)) for i in range(4)]
            rs = [E(nc.sbuf_tensor(f"rs{i}", [128, 1], F32)) for i in range(4)]
            rden = [E(nc.sbuf_tensor(f"rdenA{i}", [128, 1], F32)) for i in range(4)]
            PSC = E(nc.psum_tensor("PSC", [128, 2048], F32))
            PT = [E(nc.psum_tensor(f"PT{i}", [128, 1024], BF16)) for i in range(2)]
            POT = [E(nc.psum_tensor(f"POT{i}", [128, 512], F32)) for i in range(2)]
            sems = dram["sems"]
            s_qk = sems["b_qk4"]
            s_od = sems["b_od"]
            c_act = sems["a_act"]
            c_dve = sems["a_dve"]
            c_pe = sems["a_pe"]
            block = E(nc.Block())

            qk_rec = {}
            head_pe_done = {}

            def load_head(h):
                @block.sync
                def _(sy, h=h):
                    if h >= 4:
                        sy.wait_ge(c_pe.h, head_pe_done[h - 4])
                    i = sy.dma_start(
                        qkh[h % 4][:, :].rearrange("p (w t) -> p w t", w=2),
                        qk_d.ap().rearrange("(w r) t -> r w t", w=2)[h * 128:(h + 1) * 128])
                    qk_rec[h] = s_qk[h % 4].inc(i, 16)

            for hh in range(4):
                load_head(hh)

            mask_rec, exp_rec, sum_rec = {}, {}, {}
            pt_evac_rec, oevac_rec, av_all_rec = {}, {}, {}
            sc_rec = {}
            psc_prev_exp = [None]

            def E_S(h, qc):
                kext = (qc + 1) * 128
                nkt = (kext + 511) // 512
                prev = psc_prev_exp[0]

                @block.tensor
                def _(t, h=h, qc=qc, kext=kext, nkt=nkt, prev=prev):
                    if qc == 0:
                        t.wait_ge(s_qk[h % 4].h, qk_rec[h])
                    if prev is not None:
                        t.wait_ge(c_act.h, exp_rec[prev])
                    for kt in range(nkt):
                        k0 = kt * 512
                        k1 = min(kext, k0 + 512)
                        i = t.matmul(PSC[:, k0:k1],
                                     qkh[h % 4][:, qc * 128:(qc + 1) * 128],
                                     qkh[h % 4][:, 2048 + k0: 2048 + k1],
                                     start=True, stop=True)
                    c_pe.inc(i)
                sc_rec[(h, qc)] = c_pe.n

            def E_softmax(h, qc):
                p = h % 4
                kext = (qc + 1) * 128

                @block.vector
                def _(v, h=h, qc=qc, p=p, kext=kext):
                    v.wait_ge(c_pe.h, sc_rec[(h, qc)])
                    if qc >= 1:
                        v.wait_ge(c_act.h, exp_rec[(h, qc - 1)])
                    i = v.tensor_tensor(PSC[:, kext - 128:kext],
                                        PSC[:, kext - 128:kext],
                                        masktri[:, :], ALU.add)
                    c_dve.step(v, i)
                    i = v.reduce_max(mrow[p][:, :], PSC[:, 0:kext], AX.X)
                    c_dve.step(v, i)
                    i = v.tensor_scalar(negm[p][:, :], mrow[p][:, :], -1.0, None, ALU.mult)
                    mask_rec[(h, qc)] = c_dve.inc(i)

                @block.scalar
                def _(sc, h=h, qc=qc, p=p, kext=kext):
                    sc.wait_ge(c_dve.h, mask_rec[(h, qc)])
                    if qc >= 1:
                        sc.wait_ge(c_dve.h, sum_rec[(h, qc - 1)])
                        sc.wait_ge(c_pe.h, av_all_rec[(h, qc - 1)])
                    i = sc.activation(pbf[p][:, 0:kext], PSC[:, 0:kext],
                                      AF.Exp, bias=negm[p][:, :], accum_out=rs[p][:, :])
                    exp_rec[(h, qc)] = c_act.inc(i)
                psc_prev_exp[0] = (h, qc)

                @block.vector
                def _(v, h=h, qc=qc, p=p):
                    v.wait_ge(c_act.h, exp_rec[(h, qc)])
                    i = v.reciprocal(rden[p][:, :], rs[p][:, :])
                    sum_rec[(h, qc)] = c_dve.inc(i)

            def E_TAV(h, qc):
                p = h % 4
                sl = h % 2
                nb = (qc + 4) // 4
                pt_prev = pt_last.get(sl)
                for b in range(nb):
                    kc0, kc1 = b * 4, min(qc + 1, b * 4 + 4)

                    @block.tensor
                    def _(t, h=h, qc=qc, p=p, sl=sl, b=b, kc0=kc0, kc1=kc1, pt_prev=pt_prev):
                        if b == 0:
                            t.wait_ge(c_act.h, exp_rec[(h, qc)])
                            if pt_prev is not None:
                                t.wait_ge(c_dve.h, pt_prev)
                        else:
                            t.wait_ge(c_dve.h, pt_evac_rec[(h, qc, b - 1)])
                        for kc in range(kc0, kc1):
                            i = t.transpose(PT[sl][:, (kc % 4) * 128:(kc % 4 + 1) * 128],
                                            pbf[p][:, kc * 128:(kc + 1) * 128],
                                            identbf[:, :])
                        c_pe.inc(i)

                    tr_rec = c_pe.n

                    @block.vector
                    def _(v, h=h, qc=qc, p=p, sl=sl, b=b, kc0=kc0, kc1=kc1, tr_rec=tr_rec):
                        v.wait_ge(c_pe.h, tr_rec)
                        if qc >= 1 and b == 0:
                            v.wait_ge(c_pe.h, av_all_rec[(h, qc - 1)])
                        i = v.tensor_copy(ptsb[p][:, kc0 * 128:kc1 * 128],
                                          PT[sl][:, 0:(kc1 - kc0) * 128])
                        pt_evac_rec[(h, qc, b)] = c_dve.inc(i)
                pt_last[sl] = pt_evac_rec[(h, qc, nb - 1)]

                pot_prev = pot_last.get(sl)

                @block.tensor
                def _(t, h=h, qc=qc, p=p, sl=sl, pot_prev=pot_prev):
                    if pot_prev is not None:
                        t.wait_ge(c_dve.h, pot_prev)
                    for kc in range(qc + 1):
                        if kc % 4 == 0:
                            t.wait_ge(c_dve.h, pt_evac_rec[(h, qc, kc // 4)])
                        i = t.matmul(POT[sl][:, 0:128],
                                     ptsb[p][:, kc * 128:(kc + 1) * 128],
                                     vtm[:, kc * 2048 + h * 128: kc * 2048 + (h + 1) * 128],
                                     start=(kc == 0), stop=(kc == qc))
                        c_pe.inc(i)
                    av_all_rec[(h, qc)] = c_pe.n
                    if qc == 15:
                        head_pe_done[h] = c_pe.n

                av_done = c_pe.n

                @block.vector
                def _(v, h=h, qc=qc, p=p, sl=sl, av_done=av_done):
                    v.wait_ge(c_pe.h, av_done)
                    v.wait_ge(c_dve.h, sum_rec[(h, qc)])
                    i = v.tensor_scalar(
                        osb[:, qc * 2048 + h * 128: qc * 2048 + (h + 1) * 128],
                        POT[sl][:, 0:128], rden[p][:, :], None, ALU.mult)
                    oevac_rec[(h, qc)] = c_dve.inc(i)
                pot_last[sl] = oevac_rec[(h, qc)]

            pt_last, pot_last = {}, {}
            for hg in range(NH // 4):
                hs = [4 * hg + i for i in range(4)]
                for qc in range(16):
                    E_S(hs[0], qc)
                    E_softmax(hs[0], qc)
                    if qc >= 1:
                        E_TAV(hs[2], qc - 1)
                    E_S(hs[1], qc)
                    E_softmax(hs[1], qc)
                    if qc >= 1:
                        E_TAV(hs[3], qc - 1)
                    E_S(hs[2], qc)
                    E_softmax(hs[2], qc)
                    E_TAV(hs[0], qc)
                    E_S(hs[3], qc)
                    E_softmax(hs[3], qc)
                    E_TAV(hs[1], qc)
                E_TAV(hs[2], 15)
                E_TAV(hs[3], 15)
                if hg + 1 < NH // 4:
                    for nh_ in range(4):
                        load_head(4 * (hg + 1) + nh_)

            o_all = c_dve.n
            for g4 in range(4):
                @block.sync
                def _(sy, g4=g4):
                    if g4 == 0:
                        sy.wait_ge(c_dve.h, o_all)
                    i = sy.dma_start(
                        od_d.ap().rearrange("(qc p) dv -> p qc dv", p=128)[:, g4 * 4:(g4 + 1) * 4, :],
                        osb[:, :].rearrange("p (qc dv) -> p qc dv", dv=2048)[:, g4 * 4:(g4 + 1) * 4, :])
                    s_od.inc(i, 16)

            @block.sync
            def _(sy):
                sy.wait_ge(s_od.h, s_od.n)

        with ExitStack() as ctx:
            E = ctx.enter_context
            ofm = E(nc.sbuf_tensor("ofm", [128, 32768], BF16))
            wsl = [E(nc.sbuf_tensor(f"wslC{i}", [128, 4096], BF16)) for i in range(2)]
            oat = [E(nc.sbuf_tensor(f"oatC{i}", [128, 2048], BF16)) for i in range(2)]
            xch5 = E(nc.sbuf_tensor("xch5", [128, 2048], F32))
            ot5 = [E(nc.sbuf_tensor(f"ot5{i}", [128, 2048], BF16)) for i in range(2)]
            PO5 = [E(nc.psum_tensor(f"PO5{i}", [128, 512], F32)) for i in range(2)]
            sems = dram["sems"]
            s_ot = sems["c_ot"]
            s_wo = sems["c_wo"]
            s_oa = sems["c_oa"]
            s_tt = sems["c_tt"]
            s_ld5 = sems["a_ld"]
            s_st5 = sems["c_st"]
            c_act = sems["a_act"]
            c_dve = sems["a_dve"]
            c_pe = sems["a_pe"]
            block = E(nc.Block())

            for dc in range(16):
                @block.sync
                def _(sy, dc=dc):
                    i = sy.dma_start(ofm[:, dc * 2048:(dc + 1) * 2048],
                                     od_d.ap()[0:S, dc * 128:(dc + 1) * 128],
                                     transpose=True)
                    s_ot.inc(i, 16)
            ot_all = s_ot.n

            wo_rec = {}
            pe_wo_done = {}
            oa_rec = {}
            evac5_rec = {}
            for g in range(8):
                @block.gpsimd
                def _(gp, g=g):
                    if g >= 2:
                        gp.wait_ge(c_pe.h, pe_wo_done[g - 2])
                    i = gp.dma_start(wsl[g % 2][:, :],
                                     wot_d.ap()[g].rearrange("p dc e -> p (dc e)"))
                    wo_rec[g] = s_wo[g % 2].inc(i, 16)

                for dco2 in range(2):
                    dch = g * 2 + dco2
                    for tt in range(4):
                        kk = dch * 4 + tt

                        @block.tensor
                        def _(t, g=g, dco2=dco2, dch=dch, tt=tt, kk=kk):
                            if dco2 == 0 and tt == 0:
                                t.wait_ge(s_wo[g % 2].h, wo_rec[g])
                                if g == 0:
                                    t.wait_ge(s_ot.h, ot_all)
                            if kk >= 2:
                                t.wait_ge(c_dve.h, evac5_rec.get(kk - 2, 0))
                            for dc in range(16):
                                i = t.matmul(
                                    PO5[kk % 2][:, :],
                                    wsl[g % 2][:, dc * 256 + dco2 * 128: dc * 256 + dco2 * 128 + 128],
                                    ofm[:, dc * 2048 + tt * 512: dc * 2048 + (tt + 1) * 512],
                                    start=(dc == 0), stop=(dc == 15))
                            c_pe.inc(i)
                            if dco2 == 1 and tt == 3:
                                pe_wo_done[g] = c_pe.n

                        pk_done = c_pe.n

                        @block.vector
                        def _(v, dch=dch, tt=tt, kk=kk, pk_done=pk_done):
                            v.wait_ge(c_pe.h, pk_done)
                            if kk >= 2:
                                v.wait_ge(s_oa[kk % 2].h, oa_rec.get(kk - 2, 0))
                            i = v.tensor_scalar(oat[kk % 2][:, tt * 512:(tt + 1) * 512],
                                                PO5[kk % 2][:, :],
                                                boc[:, dch:dch + 1], None, ALU.add)
                            evac5_rec[kk] = c_dve.inc(i)

                        @block.sync
                        def _(sy, dch=dch, tt=tt, kk=kk):
                            sy.wait_ge(c_dve.h, evac5_rec[kk])
                            i = sy.dma_start(
                                oacc_d.ap()[dch * 128:(dch + 1) * 128,
                                            tt * 512:(tt + 1) * 512],
                                oat[kk % 2][:, tt * 512:(tt + 1) * 512])
                            oa_rec[kk] = s_oa[kk % 2].inc(i, 16)

            oa_all = [s_oa[0].n, s_oa[1].n]

            ot_rec5, ld_rec5, add_rec5, st_rec5 = {}, {}, {}, {}
            for tc in range(16):
                @block.sync
                def _(sy, tc=tc):
                    if tc == 0:
                        sy.wait_ge(s_oa[0].h, oa_all[0])
                        sy.wait_ge(s_oa[1].h, oa_all[1])
                    if tc >= 2:
                        sy.wait_ge(s_st5.h, st_rec5[tc - 2])
                    i = sy.dma_start(ot5[tc % 2][:, :],
                                     oacc_d.ap()[:, tc * 128:(tc + 1) * 128],
                                     transpose=True)
                    ot_rec5[tc] = s_tt.inc(i, 16)
                    if tc >= 1:
                        sy.wait_ge(s_st5.h, st_rec5[tc - 1])
                    i = sy.dma_start(xch5[:, :],
                                     src_ap[tc * 128:(tc + 1) * 128, :])
                    ld_rec5[tc] = s_ld5.inc(i, 16)

                @block.vector
                def _(v, tc=tc):
                    v.wait_ge(s_tt.h, ot_rec5[tc])
                    v.wait_ge(s_ld5.h, ld_rec5[tc])
                    i = v.tensor_tensor(xch5[:, :], xch5[:, :],
                                        ot5[tc % 2][:, :], ALU.add)
                    add_rec5[tc] = c_dve.inc(i)

                @block.sync
                def _(sy, tc=tc):
                    sy.wait_ge(c_dve.h, add_rec5[tc])
                    i = sy.dma_start(dst_ap[tc * 128:(tc + 1) * 128, :],
                                     xch5[:, :])
                    st_rec5[tc] = s_st5.inc(i, 16)

            @block.sync
            def _(sy):
                sy.wait_ge(s_st5.h, s_st5.n)


def build(n_sub=N_SUB, dbg=False, n_exp=NE, halves=2):
    nc = bass.Bass("TRN2", target_bir_lowering=False)

    x_d = nc.dram_tensor("x", [S, D], F32, kind="ExternalInput")
    wg_d = nc.dram_tensor("wg", [NE, 16, 128, 16, 128], F32, kind="ExternalInput")
    wu_d = nc.dram_tensor("wu", [NE, 16, 128, 16, 128], F32, kind="ExternalInput")
    wd_d = nc.dram_tensor("wd", [NE, 128, 16, 2048], F32, kind="ExternalInput")
    wr_d = nc.dram_tensor("wr", [128, 16, 8], F32, kind="ExternalInput")
    wqt_d = nc.dram_tensor("wqt", [24, 128, 16, 256], F32, kind="ExternalInput")
    wot_d = nc.dram_tensor("wot", [8, 128, 16, 256], F32, kind="ExternalInput")
    bqc_d = nc.dram_tensor("bqcd", [128, 48], F32, kind="ExternalInput")
    boc_d = nc.dram_tensor("bocd", [128, 16], F32, kind="ExternalInput")
    bqr_d = nc.dram_tensor("bqrd", [1, 6144], F32, kind="ExternalInput")
    mtri_d = nc.dram_tensor("mtri", [128, 128], F32, kind="ExternalInput")
    sel_d = nc.dram_tensor("sel", [8, 8, 128], F32, kind="ExternalInput")
    ident_d = nc.dram_tensor("ident", [128, 128], F32, kind="ExternalInput")
    y_d = nc.dram_tensor("y", [S, D], F32, kind="ExternalOutput")

    s1_d = nc.dram_tensor("s1", [S, D], F32, kind="Internal")
    s2_d = nc.dram_tensor("s2", [S, D], F32, kind="Internal")
    xln_d = nc.dram_tensor("xlnd", [S, D], BF16, kind="Internal")
    oacc_d = nc.dram_tensor("oaccd", [S, D], BF16, kind="Internal")
    qk_d = nc.dram_tensor("qkd", [4096, 2048], BF16, kind="Internal")
    od_d = nc.dram_tensor("odd", [S, D], BF16, kind="Internal")
    wgc_d = nc.dram_tensor("wgc", [NE, 16, 128, 2048], BF16, kind="Internal")
    wuc_d = nc.dram_tensor("wuc", [NE, 16, 128, 2048], BF16, kind="Internal")
    wdc_d = nc.dram_tensor("wdc", [NE, 128, 32768], BF16, kind="Internal")

    dram = dict(xln=xln_d, oacc=oacc_d, wg=wg_d, wu=wu_d, wd=wd_d,
                qk=qk_d, od=od_d, wqt=wqt_d, wot=wot_d,
                wgc=wgc_d, wuc=wuc_d, wdc=wdc_d)

    with ExitStack() as ctx:
        E = ctx.enter_context
        sems = {}
        for nm in ["f_ld", "f_xw", "f_xt", "f_wd", "f_wdr", "f_cwd", "f_oa", "f_ot", "f_st",
                   "f_act", "f_dve", "f_pe", "f_dbg",
                   "a_ld", "a_xw", "a_xt", "a_bv", "a_od", "a_act", "a_dve", "a_pe",
                   "b_od", "c_ot", "c_tt", "c_st"]:
            sems[nm] = Sem(ctx, nc, nm)
        for nm in ["f_wg", "f_wu", "f_wgr", "f_wur", "f_cwg", "f_cwu", "a_qs", "a_qw",
                   "c_wo", "c_oa"]:
            sems[nm] = [Sem(ctx, nc, f"{nm}{i}") for i in range(2)]
        sems["b_qk4"] = [Sem(ctx, nc, f"bqk4{i}") for i in range(4)]
        dram["sems"] = sems
        identf32 = E(nc.sbuf_tensor("identf32", [128, 128], F32))
        wrb = E(nc.sbuf_tensor("wrb", [128, 128], BF16))
        self32 = E(nc.sbuf_tensor("self32", [8, 1024], BF16))
        neginf8 = E(nc.sbuf_tensor("neginf8", [128, 8], F32))
        epsc = E(nc.sbuf_tensor("epsc", [128, 1], F32))
        identbf = E(nc.sbuf_tensor("identbf", [128, 128], BF16))
        masktri = E(nc.sbuf_tensor("masktri", [128, 128], F32))
        bqc = E(nc.sbuf_tensor("bqc", [128, 48], F32))
        bqs = E(nc.sbuf_tensor("bqs", [128, 16], F32))
        boc = E(nc.sbuf_tensor("boc", [128, 16], F32))
        s_c = Sem(ctx, nc, "sconst")
        s_cg = Sem(ctx, nc, "sconstg")

        with nc.Block() as blk0:
            @blk0.sync
            def _(sy):
                i = sy.dma_start(identf32[:, :], ident_d.ap()[:, :])
                s_c.inc(i, 16)
                i = sy.dma_start(masktri[:, :], mtri_d.ap()[:, :])
                s_c.inc(i, 16)
                i = sy.dma_start(bqc[:, :], bqc_d.ap()[:, :])
                s_c.inc(i, 16)
                i = sy.dma_start(boc[:, :], boc_d.ap()[:, :])
                s_c.inc(i, 16)
                sy.wait_ge(s_c.h, s_c.n)

            @blk0.gpsimd
            def _(g):
                i = g.dma_start(wrb[:, :],
                                wr_d.ap().rearrange("p dc e -> p (dc e)"))
                s_cg.inc(i, 16)
                i = g.dma_start(self32[:, :].rearrange("k (e i) -> k e i", i=128),
                                sel_d.ap().rearrange("e k i -> k e i"))
                s_cg.inc(i, 16)
                i = g.dma_start(identbf[:, :], ident_d.ap()[:, :])
                s_cg.inc(i, 16)
                g.memset(neginf8[:, :], NEG)
                g.memset(epsc[:, :], 1e-5)
                g.wait_ge(s_cg.h, s_cg.n)

            @blk0.vector
            def _(v):
                v.wait_ge(s_c.h, s_c.n)
                v.tensor_scalar(bqs[:, :], bqc[:, 0:16], ISQ, None, ALU.mult)

        consts = dict(identf32=identf32, self32=self32, wrb=wrb, neginf8=neginf8,
                      epsc=epsc, identbf=identbf, masktri=masktri, bqc=bqc, bqs=bqs,
                      boc=boc, bqr_d=bqr_d)

        dbgd = None
        if dbg:
            dbgd = dict(
                xlf=nc.dram_tensor("dbg_xlf", [128, 16384], BF16, kind="ExternalOutput"),
                lg=nc.dram_tensor("dbg_lg", [128, 64], F32, kind="ExternalOutput"),
                w=nc.dram_tensor("dbg_w", [128, 64], F32, kind="ExternalOutput"),
                oac=nc.dram_tensor("dbg_oac", [128, 16384], BF16, kind="ExternalOutput"),
            )
        dst1 = y_d if n_sub == 1 else s1_d
        ffn_pair(nc, consts, dram, x_d.ap(), dst1.ap(), "f1", ('write', 'read'),
                 next_ln_src=(s1_d.ap() if n_sub >= 2 else None))
        dram["ln2_xw_n"] = dram["sems"]["f_xw"].n

        if n_sub >= 2:
            dst2 = y_d if n_sub == 2 else s2_d
            attn_sublayer(nc, consts, dram, s1_d.ap(), dst2.ap(), skip_ln=True)

        if n_sub >= 3:
            ffn_pair(nc, consts, dram, s2_d.ap(), y_d.ap(), "f3", ('read', 'read'))

    return nc


def _relayout(inputs):
    f32 = np.float32
    x = np.ascontiguousarray(np.asarray(inputs["L0_attn_x"], dtype=f32))
    Wg = np.asarray(inputs["L0_ffn_Wg"], dtype=f32)
    Wu = np.asarray(inputs["L0_ffn_Wu"], dtype=f32)
    Wd = np.asarray(inputs["L0_ffn_Wd"], dtype=f32)
    Wr = np.asarray(inputs["L0_ffn_Wr"], dtype=f32)
    shared = {}
    shared["wg"] = np.ascontiguousarray(
        Wg.reshape(8, 16, 128, 16, 128).transpose(0, 3, 2, 1, 4))
    shared["wu"] = np.ascontiguousarray(
        Wu.reshape(8, 16, 128, 16, 128).transpose(0, 3, 2, 1, 4))
    shared["wd"] = np.ascontiguousarray(
        Wd.reshape(8, 16, 128, 2048).transpose(0, 2, 1, 3))
    shared["wr"] = np.ascontiguousarray(Wr.reshape(16, 128, 8).transpose(1, 0, 2))
    sel = np.zeros((8, 8, 128), dtype=f32)
    for e in range(8):
        sel[e, e, :] = 1.0
    shared["sel"] = sel
    shared["ident"] = np.ascontiguousarray(np.eye(128, dtype=f32))
    wqkv = np.asarray(inputs["L0_attn_w_qkv"], dtype=f32)
    wout = np.asarray(inputs["L0_attn_w_out"], dtype=f32)
    bqkv = np.asarray(inputs["L0_attn_b_qkv"], dtype=f32)
    bout = np.asarray(inputs["L0_attn_b_out"], dtype=f32)
    shared["wqt"] = np.ascontiguousarray(
        wqkv.reshape(24, 256, 16, 128).transpose(0, 3, 2, 1))
    shared["wot"] = np.ascontiguousarray(
        wout.reshape(8, 256, 16, 128).transpose(0, 3, 2, 1))
    shared["bqcd"] = np.ascontiguousarray(bqkv.reshape(48, 128).T)
    shared["bocd"] = np.ascontiguousarray(bout.reshape(16, 128).T)
    shared["bqrd"] = np.ascontiguousarray(bqkv.reshape(1, 6144))
    mtri = np.zeros((128, 128), dtype=f32)
    for i_ in range(128):
        mtri[i_, i_ + 1:] = NEG
    shared["mtri"] = mtri
    return x, shared


def kernel(**inputs):
    x, shared = _relayout(inputs)
    nc = build(N_SUB)
    in_maps = [dict(x=np.ascontiguousarray(x[c]), **shared) for c in range(8)]
    res = run_bass_kernel_spmd(nc, in_maps, core_ids=list(range(8)))
    return np.stack([res.results[c]["y"] for c in range(8)], axis=0)
